# version 15
# speedup vs baseline: 1.2292x; 1.0024x over previous
import numpy as np
import ml_dtypes
from contextlib import ExitStack

import concourse.bass as bass
import concourse.mybir as mybir
from concourse.bass_utils import run_bass_kernel_spmd

F32 = mybir.dt.float32
BF16 = mybir.dt.bfloat16
AX = mybir.AxisListType.X
AF = mybir.ActivationFunctionType
ALU = mybir.AluOpType

MAGIC = float(1.5 * 2.0 ** 23)
THRC = 0.5 + 2.0 ** -11
FEPS = float(np.finfo(np.float32).eps)
N_CORES = 8

CFG_FULL = dict(D=2048, DI=4096, H=8192, S=2048, NSEQ=2, HD=64, G=8, R=4,
                LN_EPS=1e-5, NMM=512)
CFG_TINY = dict(D=256, DI=256, H=512, S=128, NSEQ=2, HD=64, G=2, R=2,
                LN_EPS=1e-5, NMM=128)
CFG_MID = dict(D=256, DI=512, H=512, S=256, NSEQ=2, HD=64, G=2, R=2,
               LN_EPS=1e-5, NMM=128)


def _derive(cfg):
    c = dict(cfg)
    c["E0"] = 2 * c["DI"]
    c["KV"] = c["G"] * c["HD"]
    c["EQKV"] = c["D"] + 2 * c["KV"]
    c["TN"] = c["NSEQ"] * c["S"]
    c["nD"] = c["D"] // 128
    c["nDI"] = c["DI"] // 128
    c["nE0"] = c["E0"] // 128
    c["nH"] = c["H"] // 128
    c["nT"] = c["TN"] // 128
    c["nQK"] = (c["D"] + c["KV"]) // 128
    c["nQKV"] = c["EQKV"] // 128
    c["nEV"] = c["KV"] // 128
    c["TOKB"] = c["NMM"]
    c["Q4"] = 2 * c["TOKB"]
    c["nQ4"] = c["TN"] // c["Q4"]
    c["nBQ"] = 2
    c["nDB"] = c["D"] // c["NMM"]
    c["QB"] = c["NMM"]
    c["nQB"] = c["S"] // c["QB"]
    c["KC"] = c["S"] // 128
    c["nTS"] = c["S"] // 128
    c["HEADS"] = c["G"] * c["R"]
    c["nG1"] = c["D"] // 64
    c["nG2"] = c["H"] // 64
    return c


def build(cfg, taps=(), detect_races=False):
    c = _derive(cfg)
    D, DI, H, S = c["D"], c["DI"], c["H"], c["S"]
    TN, nD, nDI, nE0, nH, nT = c["TN"], c["nD"], c["nDI"], c["nE0"], c["nH"], c["nT"]
    NMM, TOKB, Q4, nQ4, nBQ, nDB = c["NMM"], c["TOKB"], c["Q4"], c["nQ4"], c["nBQ"], c["nDB"]
    QB, nQB, KC, G, R, HD, KV = c["QB"], c["nQB"], c["KC"], c["G"], c["R"], c["HD"], c["KV"]
    EQKV, nQK, nQKV = c["EQKV"], c["nQK"], c["nQKV"]
    NSEQ, nTS = c["NSEQ"], c["nTS"]
    nG1, nG2 = c["nG1"], c["nG2"]
    LN_EPS = c["LN_EPS"]

    nc = bass.Bass("TRN2", target_bir_lowering=False, debug=False,
                   detect_race_conditions=detect_races)

    X = nc.dram_tensor("x", [TN, D], F32, kind="ExternalInput").ap()
    WIN = nc.dram_tensor("w_inT", [nE0, nD, 128, 128], F32, kind="ExternalInput").ap()
    WOUT0 = nc.dram_tensor("w_outT0", [nDI, 128, D], F32, kind="ExternalInput").ap()
    W1 = nc.dram_tensor("w1", [H, D], F32, kind="ExternalInput").ap()
    W2 = nc.dram_tensor("w2", [D, H], F32, kind="ExternalInput").ap()
    DT = nc.dram_tensor("dt", [DI], F32, kind="ExternalInput").ap()
    GAM = nc.dram_tensor("gamma", [D], F32, kind="ExternalInput").ap()
    BET = nc.dram_tensor("beta", [D], F32, kind="ExternalInput").ap()
    WQKV = nc.dram_tensor("wqkvT", [nQKV, nD, 128, 128], F32, kind="ExternalInput").ap()
    BQKV = nc.dram_tensor("bqkv", [EQKV], F32, kind="ExternalInput").ap()
    WOUT1 = nc.dram_tensor("w_outT1", [nD, 128, D], F32, kind="ExternalInput").ap()
    BOUT1 = nc.dram_tensor("bout1", [D], F32, kind="ExternalInput").ap()
    WFC = nc.dram_tensor("wfc", [H, D], F32, kind="ExternalInput").ap()
    WPR = nc.dram_tensor("wproj", [D, H], F32, kind="ExternalInput").ap()
    Y = nc.dram_tensor("y", [TN, D], F32, kind="ExternalOutput").ap()

    dscr = lambda n, sh, dt=BF16: nc.dram_tensor(n, sh, dt, kind="Internal").ap()
    XN = dscr("XN", [TN, D], F32)
    XN2 = dscr("XN2", [TN, D])
    XQ0 = dscr("XQ0", [TN, D])
    OUT0T = dscr("OUT0T", [nDI, nT, 128, 128])
    T1N = dscr("T1N", [H, D])
    T1T = dscr("T1T", [nD, nH, 128, 128])
    H0T = dscr("H0T", [H, TN])
    XQ2 = dscr("XQ2", [TN, H])
    T2N = dscr("T2N", [D, H])
    T2T = dscr("T2T", [nH, nD, 128, 128])
    OF0T = dscr("OF0T", [D, TN])
    S0 = dscr("S0", [TN, D])
    X1 = dscr("X1", [TN, D], F32)
    XN1 = dscr("XN1", [TN, D])
    QT = dscr("QTs", [D, TN])
    KT = dscr("KTs", [KV, TN])
    OTT = dscr("OTT", [nD, nT, 128, 128])
    OTU = dscr("OTU", [nD, nT, 128, 128])
    LROWS = dscr("LROWS", [NSEQ * G * R, S], F32)
    RBULK = dscr("RBULK", [NSEQ * G * R, S], F32)
    S1 = dscr("S1", [TN, D])
    WQ1N = dscr("WQ1N", [H, D])
    WQ1T = dscr("WQ1T", [nD, nH, 128, 128])
    WQ2N = dscr("WQ2N", [D, H])
    WQ2T = dscr("WQ2T", [nH, nD, 128, 128])
    RT = dscr("RT", [H, TN])
    OUT1T = dscr("OUT1T", [D, TN])
    DTWD = dscr("DTWD", [DI], F32)

    class Ctr:
        def __init__(self):
            self.v = dict(SYA=0, SY0=0, SY1=0, SY2=0, SY3=0, GPA=0, GP0=0, GP1=0,
                          SCA=0, SC0=0, SC1=0, TC=0, VC=0, AC=0, GC=0,
                          VSV=0, VSS=0, VSG=0, VST=0, VSP=0)

        def inc(self, s, n=1):
            self.v[s] += n
            return self.v[s]

    C = Ctr()

    with ExitStack() as top:
        sem = {s: top.enter_context(nc.semaphore(s))
               for s in ("SYA", "SY0", "SY1", "SY2", "SY3", "GPA", "GP0", "GP1",
                         "SCA", "SC0", "SC1", "TC", "VC", "AC", "GC",
                         "VSV", "VSS", "VSG", "VST", "VSP")}

        def W(eng, s, val):
            if val > 0:
                eng.wait_ge(sem[s], val)

        def bar_waits(eng, snap):
            for s, v in snap.items():
                W(eng, s, v)

        _VSN = {"DVE": "VSV", "Activation": "VSS", "Pool": "VSG",
                "PE": "VST", "SP": "VSP"}

        def VN(eng, inst):
            key = _VSN[eng.engine.name]
            inst.then_inc(sem[key], 1)
            eng.wait_ge(sem[key], C.inc(key))

        block = top.enter_context(nc.Block())

        sb = lambda n, sh, dt=F32: top.enter_context(nc.sbuf_tensor(n, sh, dt))
        rs0 = sb("rs0", [128, nT])
        rs2 = sb("rs2", [128, nT])
        sw1 = sb("sw1", [128, nH])
        sw2 = sb("sw2", [128, nD])
        dtw_cols = sb("dtw_cols", [128, nDI])
        bqkv_cols = sb("bqkv_cols", [128, nQKV])
        onesF_1x128 = sb("onesF_1x128", [1, 128])
        onesB_1x128 = sb("onesB_1x128", [1, 128], BF16)
        onesB_1x64 = sb("onesB_1x64", [1, 64], BF16)
        onesB_64x1 = sb("onesB_64x1", [64, 1], BF16)
        onesB_128x1 = sb("onesB_128x1", [128, 1], BF16)
        epsA1 = sb("epsA1", [1, 1])

        with ExitStack() as sA:
            asb = lambda n, sh, dt=F32: sA.enter_context(nc.sbuf_tensor(n, sh, dt))
            gammaB = asb("gammaB", [128, D])
            betaB = asb("betaB", [128, D])
            grow = asb("grow", [1, D])
            brow = asb("brow", [1, D])
            dtrow = asb("dtrow", [1, DI])
            dst = asb("dst", [1, 16])
            pprep = sA.enter_context(nc.psum_tensor("pprep", [128, NMM], F32))
            pprep2 = sA.enter_context(nc.psum_tensor("pprep2", [128, NMM], F32))

            @block.sync
            def _(sync):
                sync.dma_start(out=grow[:], in_=GAM.unsqueeze(0)).then_inc(sem["SYA"], 16)
                sync.dma_start(out=brow[:], in_=BET.unsqueeze(0)).then_inc(sem["SYA"], 16)
                sync.dma_start(out=dtrow[:], in_=DT.unsqueeze(0)).then_inc(sem["SYA"], 16)
                sync.dma_start(out=bqkv_cols[:],
                               in_=BQKV.rearrange("(j p) -> j p", p=128).transpose([1, 0]),
                               allow_slow_non_contiguous=True).then_inc(sem["SYA"], 16)
            C.inc("SYA", 64)
            sy_prep = C.v["SYA"]

            @block.vector
            def _(v):
                v.memset(onesF_1x128[:], 1.0)
                v.memset(onesB_1x128[:], 1.0)
                v.memset(onesB_1x64[:], 1.0)
                v.memset(onesB_64x1[:], 1.0)
                v.memset(onesB_128x1[:], 1.0)
                v.memset(epsA1[:], FEPS).then_inc(sem["VC"], 1)
                W(v, "SYA", sy_prep)
                v.tensor_reduce(out=dst[:, 0:1], in_=dtrow[:], axis=AX, op=ALU.min
                                ).then_inc(sem["VSV"], 1)
                v.wait_ge(sem["VSV"], C.inc("VSV"))
                v.tensor_scalar(out=dst[:, 1:2], in0=dst[:, 0:1], scalar1=1.0,
                                scalar2=None, op0=ALU.mult).then_inc(sem["VC"], 1)
            C.inc("VC", 2)
            vc_dt1 = C.v["VC"]

            @block.scalar
            def _(s):
                W(s, "VC", vc_dt1)
                s.activation(out=dtrow[:], in_=dtrow[:], func=AF.Exp, scale=-1.0,
                             bias=dst[:, 1:2]).then_inc(sem["AC"], 1)
            C.inc("AC")
            ac_dt = C.v["AC"]

            @block.vector
            def _(v):
                W(v, "AC", ac_dt)
                v.reduce_sum(out=dst[:, 2:3], in_=dtrow[:], axis=AX).then_inc(sem["VC"], 1)
            C.inc("VC")
            vc_dt2 = C.v["VC"]

            @block.scalar
            def _(s):
                W(s, "VC", vc_dt2)
                VN(s, s.activation(out=dst[:, 3:4], in_=dst[:, 2:3], func=AF.Ln))
                s.activation(out=dst[:, 4:5], in_=dst[:, 3:4], func=AF.Exp,
                             scale=-1.0).then_inc(sem["AC"], 1)
            C.inc("AC")
            ac_dt2 = C.v["AC"]

            @block.vector
            def _(v):
                W(v, "AC", ac_dt2)
                VN(v, v.tensor_tensor(out=dst[:, 5:6], in0=dst[:, 4:5], in1=dst[:, 2:3], op=ALU.mult))
                VN(v, v.tensor_scalar(out=dst[:, 5:6], in0=dst[:, 5:6], scalar1=-1.0,
                                      scalar2=2.0, op0=ALU.mult, op1=ALU.add))
                VN(v, v.tensor_tensor(out=dst[:, 6:7], in0=dst[:, 4:5], in1=dst[:, 5:6], op=ALU.mult))
                v.tensor_scalar(out=dtrow[:], in0=dtrow[:], scalar1=dst[:, 6:7],
                                scalar2=None, op0=ALU.mult).then_inc(sem["VC"], 1)
            C.inc("VC")
            vc_dt3 = C.v["VC"]

            @block.scalar
            def _(s):
                W(s, "VC", vc_dt3)
                s.dma_start(out=DTWD.unsqueeze(0), in_=dtrow[:]).then_inc(sem["SCA"], 16)
            C.inc("SCA", 16)
            sc_dtw = C.v["SCA"]

            @block.sync
            def _(sync):
                W(sync, "SCA", sc_dtw)
                sync.dma_start(out=dtw_cols[:],
                               in_=DTWD.rearrange("(j p) -> j p", p=128).transpose([1, 0]),
                               allow_slow_non_contiguous=True).then_inc(sem["SYA"], 16)
            C.inc("SYA", 16)

            nDBl = D // NMM
            tcP0 = C.v["TC"]
            vcP0 = C.v["VC"]

            @block.tensor
            def _(t):
                W(t, "SYA", sy_prep)
                W(t, "VC", vc_dt1)
                for i in range(nDBl):
                    if i >= 1:
                        W(t, "VC", vcP0 + i)
                    t.matmul(out=pprep[:, :], lhsT=onesF_1x128[:],
                             rhs=grow[:, i * NMM:(i + 1) * NMM], start=True, stop=True)
                    t.matmul(out=pprep2[:, :], lhsT=onesF_1x128[:],
                             rhs=brow[:, i * NMM:(i + 1) * NMM], start=True, stop=True
                             ).then_inc(sem["TC"], 1)
            C.inc("TC", nDBl)

            @block.vector
            def _(v):
                for i in range(nDBl):
                    W(v, "TC", tcP0 + i + 1)
                    v.tensor_copy(out=gammaB[:, i * NMM:(i + 1) * NMM], in_=pprep[:, :])
                    v.tensor_copy(out=betaB[:, i * NMM:(i + 1) * NMM], in_=pprep2[:, :]
                                  ).then_inc(sem["VC"], 1)
            C.inc("VC", nDBl)

            barP = dict(C.v)

            xA = [asb(f"xA{i}", [128, D]) for i in range(2)]
            xnA = [asb(f"xnA{i}", [128, D]) for i in range(2)]
            tmpA = asb("tmpA", [128, D])
            xn2A = [asb(f"xn2A{i}", [128, D], BF16) for i in range(2)]
            xq0A = [asb(f"xq0A{i}", [128, D], BF16) for i in range(2)]
            tqA = asb("tqA", [128, D])
            sqjunk = [asb(f"sqjunkA{i}", [128, D]) for i in range(2)]
            stA = [asb(f"stA{i}", [128, 16]) for i in range(2)]
            ltA = asb("ltA", [128, 1])

            syA0 = [C.v["SY0"], C.v["SY1"]]
            scA0 = [C.v["SC0"], C.v["SC1"]]

            @block.sync
            def _(sync):
                bar_waits(sync, barP)
                for t in range(nT):
                    b = t % 2
                    if t >= 2:
                        W(sync, "VC", barP["VC"] + 3 * (t - 1))
                        W(sync, "AC", barP["AC"] + 3 * (t - 1))
                    sync.dma_start(out=xA[b][:], in_=X[t * 128:(t + 1) * 128, :]
                                   ).then_inc(sem["SY" + str(b)], 16)
            C.inc("SY0", 16 * ((nT + 1) // 2))
            C.inc("SY1", 16 * (nT // 2))

            @block.vector
            def _(v):
                bar_waits(v, barP)
                for t in range(nT):
                    b = t % 2
                    W(v, "SY" + str(b), syA0[b] + 16 * (t // 2 + 1))
                    if t >= 2:
                        W(v, "SC" + str(b), scA0[b] + 48 * ((t - 2) // 2 + 1))
                    st = stA[b]
                    v.reduce_sum(out=st[:, 0:1], in_=xA[b][:], axis=AX
                                 ).then_inc(sem["VSV"], 1)
                    v.wait_ge(sem["VSV"], C.inc("VSV"))
                    W(v, "AC", barP["AC"] + 3 * t + 1)
                    v.reduce_sum(out=st[:, 1:2], in_=sqjunk[b][:], axis=AX
                                 ).then_inc(sem["VSV"], 1)
                    v.wait_ge(sem["VSV"], C.inc("VSV"))
                    VN(v, v.tensor_scalar(out=st[:, 2:3], in0=st[:, 0:1], scalar1=1.0 / D,
                                          scalar2=None, op0=ALU.mult))
                    VN(v, v.tensor_scalar(out=st[:, 3:4], in0=st[:, 1:2], scalar1=1.0 / D,
                                          scalar2=None, op0=ALU.mult))
                    VN(v, v.tensor_tensor(out=st[:, 4:5], in0=st[:, 2:3], in1=st[:, 2:3], op=ALU.mult))
                    VN(v, v.tensor_tensor(out=st[:, 5:6], in0=st[:, 3:4], in1=st[:, 4:5], op=ALU.subtract))
                    v.tensor_scalar(out=st[:, 6:7], in0=st[:, 5:6], scalar1=LN_EPS,
                                    scalar2=None, op0=ALU.add).then_inc(sem["VC"], 1)
                    W(v, "AC", barP["AC"] + 3 * t + 2)
                    VN(v, v.tensor_tensor(out=st[:, 8:9], in0=st[:, 7:8], in1=st[:, 7:8], op=ALU.mult))
                    VN(v, v.tensor_tensor(out=st[:, 9:10], in0=st[:, 8:9], in1=st[:, 6:7], op=ALU.mult))
                    VN(v, v.tensor_scalar(out=st[:, 9:10], in0=st[:, 9:10], scalar1=-0.5,
                                          scalar2=1.5, op0=ALU.mult, op1=ALU.add))
                    VN(v, v.tensor_tensor(out=st[:, 10:11], in0=st[:, 7:8], in1=st[:, 9:10], op=ALU.mult))
                    VN(v, v.tensor_scalar(out=xnA[b][:], in0=xA[b][:], scalar1=st[:, 2:3],
                                          scalar2=st[:, 10:11], op0=ALU.subtract, op1=ALU.mult))
                    VN(v, v.scalar_tensor_tensor(out=tmpA[:], in0=xnA[b][:], scalar=1.0,
                                                 in1=gammaB[:], op0=ALU.mult, op1=ALU.mult))
                    v.tensor_tensor(out=xn2A[b][:], in0=tmpA[:], in1=betaB[:], op=ALU.add)
                    v.tensor_reduce(out=st[:, 11:12], in_=xnA[b][:], axis=AX, op=ALU.max,
                                    apply_absolute_value=True).then_inc(sem["VSV"], 1)
                    v.wait_ge(sem["VSV"], C.inc("VSV"))
                    VN(v, v.tensor_scalar(out=st[:, 12:13], in0=st[:, 11:12], scalar1=1e-5,
                                          scalar2=None, op0=ALU.max))
                    v.tensor_scalar(out=rs0[:, t:t + 1], in0=st[:, 12:13], scalar1=1.0 / 127.0,
                                    scalar2=None, op0=ALU.mult).then_inc(sem["VC"], 1)
                    W(v, "AC", barP["AC"] + 3 * t + 3)
                    VN(v, v.tensor_tensor(out=st[:, 14:15], in0=st[:, 13:14], in1=st[:, 12:13], op=ALU.mult))
                    VN(v, v.tensor_scalar(out=st[:, 14:15], in0=st[:, 14:15], scalar1=-1.0,
                                          scalar2=2.0, op0=ALU.mult, op1=ALU.add))
                    VN(v, v.tensor_tensor(out=st[:, 14:15], in0=st[:, 13:14], in1=st[:, 14:15], op=ALU.mult))
                    VN(v, v.tensor_scalar(out=st[:, 15:16], in0=st[:, 14:15], scalar1=127.0,
                                          scalar2=None, op0=ALU.mult))
                    VN(v, v.tensor_scalar(out=tqA[:], in0=xnA[b][:], scalar1=st[:, 15:16],
                                          scalar2=None, op0=ALU.mult))
                    v.tensor_scalar(out=xq0A[b][:], in0=tqA[:], scalar1=MAGIC,
                                    scalar2=MAGIC, op0=ALU.add, op1=ALU.subtract
                                    ).then_inc(sem["VC"], 1)
            C.inc("VC", 3 * nT)

            @block.scalar
            def _(s):
                bar_waits(s, barP)
                for t in range(nT):
                    b = t % 2
                    W(s, "SY" + str(b), syA0[b] + 16 * (t // 2 + 1))
                    s.activation(out=sqjunk[b][:], in_=xA[b][:], func=AF.Square
                                 ).then_inc(sem["AC"], 1)
                    W(s, "VC", barP["VC"] + 3 * t + 1)
                    VN(s, s.activation(out=ltA[:], in_=stA[b][:, 6:7], func=AF.Ln))
                    s.activation(out=stA[b][:, 7:8], in_=ltA[:], func=AF.Exp,
                                 scale=-0.5).then_inc(sem["AC"], 1)
                    W(s, "VC", barP["VC"] + 3 * t + 2)
                    VN(s, s.activation(out=ltA[:], in_=stA[b][:, 12:13], func=AF.Ln))
                    s.activation(out=stA[b][:, 13:14], in_=ltA[:], func=AF.Exp,
                                 scale=-1.0).then_inc(sem["AC"], 1)
                    W(s, "VC", barP["VC"] + 3 * t + 3)
                    s.dma_start(out=XN[t * 128:(t + 1) * 128, :], in_=xnA[b][:]).then_inc(sem["SC" + str(b)], 16)
                    s.dma_start(out=XN2[t * 128:(t + 1) * 128, :], in_=xn2A[b][:]).then_inc(sem["SC" + str(b)], 16)
                    s.dma_start(out=XQ0[t * 128:(t + 1) * 128, :], in_=xq0A[b][:]).then_inc(sem["SC" + str(b)], 16)
            C.inc("AC", 3 * nT)
            C.inc("SC0", 48 * ((nT + 1) // 2))
            C.inc("SC1", 48 * (nT // 2))

        barB = dict(C.v)
        with ExitStack() as sB:
            bsb = lambda n, sh, dt=F32: sB.enter_context(nc.sbuf_tensor(n, sh, dt))
            xn2T = [bsb(f"xn2T{i}", [128, TN], BF16) for i in range(nD)]
            wbuf = [bsb(f"wbufB{i}", [128, nD * 128], BF16) for i in range(4)]
            ezc = [[bsb(f"ezc{s}{b}", [128, TOKB]) for b in range(nBQ)] for s in range(2)]
            egc = [[bsb(f"egc{s}{b}", [128, TOKB]) for b in range(nBQ)] for s in range(2)]
            esg = [[bsb(f"esg{s}{b}", [128, TOKB], BF16) for b in range(nBQ)] for s in range(2)]
            esl = [[bsb(f"esl{s}{b}", [128, TOKB], BF16) for b in range(nBQ)] for s in range(2)]
            eob = [[bsb(f"eob{s}{b}", [128, TOKB], BF16) for b in range(nBQ)] for s in range(2)]
            pz = [[sB.enter_context(nc.psum_tensor(f"pz{s}{b}", [128, TOKB], F32))
                   for b in range(nBQ)] for s in range(2)]
            pg = [[sB.enter_context(nc.psum_tensor(f"pg{s}{b}", [128, TOKB], F32))
                   for b in range(nBQ)] for s in range(2)]

            @block.sync
            def _(sync):
                bar_waits(sync, barB)
                for di in range(nD):
                    sync.dma_start(out=xn2T[di][:],
                                   in_=XN2[0:TN, di * 128:(di + 1) * 128],
                                   transpose=True).then_inc(sem["SYA"], 16)
            C.inc("SYA", 16 * nD)
            syB1 = C.v["SYA"]

            npairs = nDI
            niterB = npairs * nQ4
            tcB0, vcB0, acB0 = C.v["TC"], C.v["VC"], C.v["AC"]
            gpB0 = [C.v["GP0"], C.v["GP1"]]
            scB0 = [C.v["SC0"], C.v["SC1"]]

            @block.gpsimd
            def _(g):
                bar_waits(g, barB)
                for j in range(npairs):
                    if j >= 2:
                        W(g, "TC", tcB0 + (j - 1) * nQ4)
                    g.dma_start(out=wbuf[(2 * j) % 4][:],
                                in_=WIN[j].transpose([1, 0, 2])).then_inc(sem["GP" + str(j % 2)], 16)
                    g.dma_start(out=wbuf[(2 * j + 1) % 4][:],
                                in_=WIN[j + nDI].transpose([1, 0, 2])).then_inc(sem["GP" + str(j % 2)], 16)
            C.inc("GP0", 32 * ((npairs + 1) // 2))
            C.inc("GP1", 32 * (npairs // 2))

            @block.tensor
            def _(t):
                bar_waits(t, barB)
                W(t, "SYA", syB1)
                it = 0
                for j in range(npairs):
                    W(t, "GP" + str(j % 2), gpB0[j % 2] + 32 * (j // 2 + 1))
                    wz = wbuf[(2 * j) % 4]
                    wg = wbuf[(2 * j + 1) % 4]
                    for q4 in range(nQ4):
                        s = it % 2
                        if it >= 2:
                            W(t, "VC", vcB0 + 2 * (it - 1))
                        mm = None
                        for wt, pp in ((wz, pz[s]), (wg, pg[s])):
                            for b in range(nBQ):
                                off = q4 * Q4 + b * TOKB
                                for di in range(nD):
                                    mm = t.matmul(out=pp[b][:, :],
                                                  lhsT=wt[:, di * 128:(di + 1) * 128],
                                                  rhs=xn2T[di][:, off:off + TOKB],
                                                  start=(di == 0), stop=(di == nD - 1))
                        mm.then_inc(sem["TC"], 1)
                        it += 1
            C.inc("TC", niterB)

            @block.vector
            def _(v):
                bar_waits(v, barB)
                it = 0
                for j in range(npairs):
                    for q4 in range(nQ4):
                        s = it % 2
                        W(v, "TC", tcB0 + it + 1)
                        if it >= 2:
                            W(v, "SC" + str(s), scB0[s] + 16 * nBQ * ((it - 2) // 2 + 1))
                        for b in range(nBQ):
                            v.tensor_scalar(out=ezc[s][b][:], in0=pz[s][b][:, :], scalar1=15.0,
                                            scalar2=-15.0, op0=ALU.min, op1=ALU.max)
                            v.tensor_scalar(out=egc[s][b][:], in0=pg[s][b][:, :], scalar1=15.0,
                                            scalar2=-15.0, op0=ALU.min, op1=ALU.max)
                        v.engine_nop().then_inc(sem["VC"], 1)
                        W(v, "AC", acB0 + it + 1)
                        li = None
                        for b in range(nBQ):
                            li = v.scalar_tensor_tensor(out=ezc[s][b][:], in0=pz[s][b][:, :],
                                                        scalar=dtw_cols[:, j:j + 1], in1=esg[s][b][:],
                                                        op0=ALU.mult, op1=ALU.mult)
                        VN(v, li)
                        for b in range(nBQ):
                            v.tensor_tensor(out=eob[s][b][:], in0=ezc[s][b][:],
                                            in1=esl[s][b][:], op=ALU.mult)
                        v.engine_nop().then_inc(sem["VC"], 1)
                        it += 1
            C.inc("VC", 2 * niterB)

            @block.scalar
            def _(s_):
                bar_waits(s_, barB)
                it = 0
                for j in range(npairs):
                    for q4 in range(nQ4):
                        s = it % 2
                        W(s_, "VC", vcB0 + 2 * it + 1)
                        li = None
                        for b in range(nBQ):
                            s_.activation(out=esg[s][b][:], in_=ezc[s][b][:], func=AF.Sigmoid)
                            li = s_.activation(out=esl[s][b][:], in_=egc[s][b][:], func=AF.Silu)
                        li.then_inc(sem["AC"], 1)
                        W(s_, "VC", vcB0 + 2 * it + 2)
                        for b in range(nBQ):
                            off = q4 * Q4 + b * TOKB
                            tj0 = off // 128
                            ns = TOKB // 128
                            s_.dma_start(out=OUT0T[j, tj0:tj0 + ns].transpose([1, 0, 2]),
                                         in_=eob[s][b][:]).then_inc(sem["SC" + str(s)], 16)
                        it += 1
            C.inc("AC", niterB)
            C.inc("SC0", 16 * nBQ * ((niterB + 1) // 2))
            C.inc("SC1", 16 * nBQ * (niterB // 2))

        barB2 = dict(C.v)
        with ExitStack() as sB2:
            bsb = lambda n, sh, dt=F32: sB2.enter_context(nc.sbuf_tensor(n, sh, dt))
            wstr = [bsb(f"wstr0_{i}", [128, D], BF16) for i in range(nDI)]
            lblk = [bsb(f"lblkB2{i}", [128, nDI * 128], BF16) for i in range(2)]
            xnB = [bsb(f"xnB{i}", [128, D]) for i in range(2)]
            s0b = [bsb(f"s0b{i}", [128, D], BF16) for i in range(2)]
            pd = [[sB2.enter_context(nc.psum_tensor(f"pdB2{s}{b}", [128, NMM], F32))
                   for b in range(nDB)] for s in range(2)]

            tcB2, vcB2 = C.v["TC"], C.v["VC"]
            scB2 = [C.v["SC0"], C.v["SC1"]]
            syB2 = [C.v["SY0"], C.v["SY1"]]
            gpB2x = [C.v["GP0"], C.v["GP1"]]

            @block.gpsimd
            def _(g):
                bar_waits(g, barB2)
                for ei in range(nDI):
                    g.dma_start(out=wstr[ei][:], in_=WOUT0[ei]).then_inc(sem["GPA"], 16)
                for t in range(nT):
                    s = t % 2
                    if t >= 2:
                        W(g, "VC", vcB2 + t - 1)
                    g.dma_start(out=xnB[s][:], in_=XN[t * 128:(t + 1) * 128, :]
                                ).then_inc(sem["GP" + str(s)], 16)
            C.inc("GPA", 16 * nDI)
            gpB2 = C.v["GPA"]
            C.inc("GP0", 16 * ((nT + 1) // 2))
            C.inc("GP1", 16 * (nT // 2))

            @block.sync
            def _(sync):
                bar_waits(sync, barB2)
                for t in range(nT):
                    s = t % 2
                    if t >= 2:
                        W(sync, "TC", tcB2 + t - 1)
                    sync.dma_start(out=lblk[s][:],
                                   in_=OUT0T[:, t].transpose([1, 0, 2])).then_inc(sem["SY" + str(s)], 16)
            C.inc("SY0", 16 * ((nT + 1) // 2))
            C.inc("SY1", 16 * (nT // 2))

            @block.tensor
            def _(t_):
                bar_waits(t_, barB2)
                W(t_, "GPA", gpB2)
                for t in range(nT):
                    s = t % 2
                    W(t_, "SY" + str(s), syB2[s] + 16 * (t // 2 + 1))
                    if t >= 2:
                        W(t_, "VC", vcB2 + t - 1)
                    mm = None
                    for ei in range(nDI):
                        for b in range(nDB):
                            mm = t_.matmul(out=pd[s][b][:, :],
                                           lhsT=lblk[s][:, ei * 128:(ei + 1) * 128],
                                           rhs=wstr[ei][:, b * NMM:(b + 1) * NMM],
                                           start=(ei == 0), stop=(ei == nDI - 1))
                    mm.then_inc(sem["TC"], 1)
            C.inc("TC", nT)

            @block.vector
            def _(v):
                bar_waits(v, barB2)
                for t in range(nT):
                    s = t % 2
                    W(v, "TC", tcB2 + t + 1)
                    W(v, "GP" + str(s), gpB2x[s] + 16 * (t // 2 + 1))
                    if t >= 2:
                        W(v, "SC" + str(s), scB2[s] + 16 * ((t - 2) // 2 + 1))
                    for b in range(nDB):
                        v.tensor_tensor(out=s0b[s][:, b * NMM:(b + 1) * NMM], in0=pd[s][b][:, :],
                                        in1=xnB[s][:, b * NMM:(b + 1) * NMM], op=ALU.add)
                    v.engine_nop().then_inc(sem["VC"], 1)
            C.inc("VC", nT)

            @block.scalar
            def _(s_):
                bar_waits(s_, barB2)
                for t in range(nT):
                    s = t % 2
                    W(s_, "VC", vcB2 + t + 1)
                    s_.dma_start(out=S0[t * 128:(t + 1) * 128, :], in_=s0b[s][:]).then_inc(sem["SC" + str(s)], 16)
            C.inc("SC0", 16 * ((nT + 1) // 2))
            C.inc("SC1", 16 * (nT // 2))

        def stage_blquant(tag, WSRC, TDST, ntile, width, swtile):
            bar = dict(C.v)
            with ExitStack() as sq:
                qsb = lambda n, sh, dt=F32: sq.enter_context(nc.sbuf_tensor(n, sh, dt))
                wt = [qsb(f"wt{tag}{i}", [128, width]) for i in range(2)]
                lt = [qsb(f"lt{tag}{i}", [128, width], BF16) for i in range(2)]
                gt = [qsb(f"gt{tag}{i}", [128, width], BF16) for i in range(2)]
                tb = [qsb(f"tb{tag}{i}", [128, width], BF16) for i in range(2)]
                qst = [qsb(f"qst{tag}{i}", [128, 8]) for i in range(2)]
                vc0 = C.v["VC"]
                sy0 = [C.v["SY0"], C.v["SY1"]]
                sc0 = [C.v["SC0"], C.v["SC1"]]

                @block.sync
                def _(sync):
                    bar_waits(sync, bar)
                    for k in range(ntile):
                        b = k % 2
                        if k >= 2:
                            W(sync, "VC", vc0 + k - 1)
                        sync.dma_start(out=wt[b][:], in_=WSRC[k * 128:(k + 1) * 128, :]
                                       ).then_inc(sem["SY" + str(b)], 16)
                C.inc("SY0", 16 * ((ntile + 1) // 2))
                C.inc("SY1", 16 * (ntile // 2))

                @block.vector
                def _(v):
                    bar_waits(v, bar)
                    for k in range(ntile):
                        b = k % 2
                        W(v, "SY" + str(b), sy0[b] + 16 * (k // 2 + 1))
                        if k >= 2:
                            W(v, "SC" + str(b), sc0[b] + 16 * ((k - 2) // 2 + 1))
                        st = qst[b]
                        v.tensor_reduce(out=st[:, 0:1], in_=wt[b][:], axis=AX, op=ALU.add,
                                        apply_absolute_value=True).then_inc(sem["VSV"], 1)
                        v.wait_ge(sem["VSV"], C.inc("VSV"))
                        VN(v, v.tensor_scalar(out=st[:, 1:2], in0=st[:, 0:1], scalar1=1.0 / width,
                                              scalar2=1e-5, op0=ALU.mult, op1=ALU.max))
                        v.tensor_copy(out=swtile[:, k:k + 1], in_=st[:, 1:2])
                        v.tensor_scalar(out=st[:, 2:3], in0=st[:, 1:2], scalar1=0.5,
                                        scalar2=None, op0=ALU.mult)
                        VN(v, v.tensor_scalar(out=st[:, 3:4], in0=st[:, 1:2], scalar1=-0.5,
                                              scalar2=None, op0=ALU.mult))
                        v.tensor_scalar(out=lt[b][:], in0=wt[b][:], scalar1=st[:, 3:4],
                                        scalar2=None, op0=ALU.is_lt)
                        VN(v, v.tensor_scalar(out=gt[b][:], in0=wt[b][:], scalar1=st[:, 2:3],
                                              scalar2=None, op0=ALU.is_gt))
                        v.tensor_tensor(out=tb[b][:], in0=gt[b][:], in1=lt[b][:],
                                        op=ALU.subtract).then_inc(sem["VC"], 1)
                C.inc("VC", ntile)

                @block.scalar
                def _(s_):
                    bar_waits(s_, bar)
                    for k in range(ntile):
                        b = k % 2
                        W(s_, "VC", vc0 + k + 1)
                        s_.dma_start(out=TDST[k * 128:(k + 1) * 128, :], in_=tb[b][:]
                                     ).then_inc(sem["SC" + str(b)], 16)
                C.inc("SC0", 16 * ((ntile + 1) // 2))
                C.inc("SC1", 16 * (ntile // 2))

        def stage_transpose(tag, NSRC, TILED, rows, cols):
            bar = dict(C.v)
            ncol = cols // 128
            with ExitStack() as st_:
                ttr = [st_.enter_context(nc.sbuf_tensor(f"ttr{tag}{i}", [128, rows], BF16))
                       for i in range(2)]
                sy0 = [C.v["SY0"], C.v["SY1"]]
                sc0 = [C.v["SC0"], C.v["SC1"]]

                @block.sync
                def _(sync):
                    bar_waits(sync, bar)
                    for j in range(ncol):
                        b = j % 2
                        if j >= 2:
                            W(sync, "SC" + str(b), sc0[b] + 16 * ((j - 2) // 2 + 1))
                        sync.dma_start(out=ttr[b][:], in_=NSRC[0:rows, j * 128:(j + 1) * 128],
                                       transpose=True).then_inc(sem["SY" + str(b)], 16)
                C.inc("SY0", 16 * ((ncol + 1) // 2))
                C.inc("SY1", 16 * (ncol // 2))

                @block.scalar
                def _(s_):
                    bar_waits(s_, bar)
                    for j in range(ncol):
                        b = j % 2
                        W(s_, "SY" + str(b), sy0[b] + 16 * (j // 2 + 1))
                        s_.dma_start(out=TILED[j].transpose([1, 0, 2]), in_=ttr[b][:]
                                     ).then_inc(sem["SC" + str(b)], 16)
                C.inc("SC0", 16 * ((ncol + 1) // 2))
                C.inc("SC1", 16 * (ncol // 2))

        def stage_featmm(tag, TILEDW, RHS_TILES, OUTD, nOUT, scale_cols):
            bar = dict(C.v)
            nK = len(RHS_TILES)
            with ExitStack() as sm:
                msb = lambda n, sh, dt=F32: sm.enter_context(nc.sbuf_tensor(n, sh, dt))
                wbk = [msb(f"wbk{tag}{i}", [128, nK * 128], BF16) for i in range(2)]
                ob = [msb(f"obf{tag}{i}", [128, Q4], BF16) for i in range(2)]
                pp = [[sm.enter_context(nc.psum_tensor(f"pp{tag}{s}{b}", [128, TOKB], F32))
                       for b in range(nBQ)] for s in range(2)]
                tc0, vc0 = C.v["TC"], C.v["VC"]
                sy0 = [C.v["SY0"], C.v["SY1"]]
                sc0 = [C.v["SC0"], C.v["SC1"]]
                niter = nOUT * nQ4

                @block.sync
                def _(sync):
                    bar_waits(sync, bar)
                    for h in range(nOUT):
                        b = h % 2
                        if h >= 2:
                            W(sync, "TC", tc0 + (h - 1) * nQ4)
                        sync.dma_start(out=wbk[b][:], in_=TILEDW[:, h].transpose([1, 0, 2])
                                       ).then_inc(sem["SY" + str(b)], 16)
                C.inc("SY0", 16 * ((nOUT + 1) // 2))
                C.inc("SY1", 16 * (nOUT // 2))

                @block.tensor
                def _(t_):
                    bar_waits(t_, bar)
                    it = 0
                    for h in range(nOUT):
                        wb = wbk[h % 2]
                        W(t_, "SY" + str(h % 2), sy0[h % 2] + 16 * (h // 2 + 1))
                        for q4 in range(nQ4):
                            s = it % 2
                            if it >= 2:
                                W(t_, "VC", vc0 + (it - 1))
                            mm = None
                            for b in range(nBQ):
                                off = q4 * Q4 + b * TOKB
                                for di in range(nK):
                                    mm = t_.matmul(out=pp[s][b][:, :],
                                                   lhsT=wb[:, di * 128:(di + 1) * 128],
                                                   rhs=RHS_TILES[di][:, off:off + TOKB],
                                                   start=(di == 0), stop=(di == nK - 1))
                            mm.then_inc(sem["TC"], 1)
                            it += 1
                C.inc("TC", niter)

                @block.vector
                def _(v):
                    bar_waits(v, bar)
                    it = 0
                    for h in range(nOUT):
                        for q4 in range(nQ4):
                            s = it % 2
                            W(v, "TC", tc0 + it + 1)
                            if it >= 2:
                                W(v, "SC" + str(s), sc0[s] + 16 * ((it - 2) // 2 + 1))
                            for b in range(nBQ):
                                v.tensor_scalar(out=ob[s][:, b * TOKB:(b + 1) * TOKB],
                                                in0=pp[s][b][:, :],
                                                scalar1=scale_cols[:, h:h + 1],
                                                scalar2=None, op0=ALU.mult)
                            v.engine_nop().then_inc(sem["VC"], 1)
                            it += 1
                C.inc("VC", niter)

                @block.scalar
                def _(s_):
                    bar_waits(s_, bar)
                    it = 0
                    for h in range(nOUT):
                        for q4 in range(nQ4):
                            s = it % 2
                            W(s_, "VC", vc0 + it + 1)
                            s_.dma_start(out=OUTD[h * 128:(h + 1) * 128, q4 * Q4:(q4 + 1) * Q4],
                                         in_=ob[s][:]).then_inc(sem["SC" + str(s)], 16)
                            it += 1
                C.inc("SC0", 16 * ((niter + 1) // 2))
                C.inc("SC1", 16 * (niter // 2))

        def stage_qmm(tag, XSRC, TILEDW, OUTD, nKC, scale_cols):
            bar = dict(C.v)
            with ExitStack() as sm:
                msb = lambda n, sh, dt=F32: sm.enter_context(nc.sbuf_tensor(n, sh, dt))
                xqT = [msb(f"xqT{tag}{i}", [128, Q4], BF16) for i in range(nKC)]
                wbk = [msb(f"wbkq{tag}{i}", [128, nKC * 128], BF16) for i in range(2)]
                ob = [msb(f"obq{tag}{i}", [128, Q4], BF16) for i in range(2)]
                pp = [[sm.enter_context(nc.psum_tensor(f"ppq{tag}{s}{b}", [128, TOKB], F32))
                       for b in range(nBQ)] for s in range(2)]
                tc0, vc0 = C.v["TC"], C.v["VC"]
                syA0 = C.v["SYA"]
                gpW0 = [C.v["GP0"], C.v["GP1"]]
                sc0 = [C.v["SC0"], C.v["SC1"]]

                @block.sync
                def _(sync):
                    bar_waits(sync, bar)
                    for q4 in range(nQ4):
                        if q4 >= 1:
                            W(sync, "TC", tc0 + q4 * nD)
                        for hj in range(nKC):
                            sync.dma_start(out=xqT[hj][:],
                                           in_=XSRC[q4 * Q4:(q4 + 1) * Q4, hj * 128:(hj + 1) * 128],
                                           transpose=True).then_inc(sem["SYA"], 16)
                C.inc("SYA", 16 * nKC * nQ4)

                @block.gpsimd
                def _(g):
                    bar_waits(g, bar)
                    for q4 in range(nQ4):
                        for dj in range(nD):
                            w = q4 * nD + dj
                            if w >= 2:
                                W(g, "TC", tc0 + w - 1)
                            g.dma_start(out=wbk[dj % 2][:],
                                        in_=TILEDW[:, dj].transpose([1, 0, 2])
                                        ).then_inc(sem["GP" + str(w % 2)], 16)
                C.inc("GP0", 16 * ((nQ4 * nD + 1) // 2))
                C.inc("GP1", 16 * ((nQ4 * nD) // 2))

                @block.tensor
                def _(t_):
                    bar_waits(t_, bar)
                    it = 0
                    for q4 in range(nQ4):
                        for dj in range(nD):
                            s = it % 2
                            w = q4 * nD + dj
                            W(t_, "SYA", syA0 + 16 * nKC * (q4 + 1))
                            W(t_, "GP" + str(w % 2), gpW0[w % 2] + 16 * (w // 2 + 1))
                            if it >= 2:
                                W(t_, "VC", vc0 + it - 1)
                            mm = None
                            for b in range(nBQ):
                                for hj in range(nKC):
                                    mm = t_.matmul(out=pp[s][b][:, :],
                                                   lhsT=wbk[dj % 2][:, hj * 128:(hj + 1) * 128],
                                                   rhs=xqT[hj][:, b * TOKB:(b + 1) * TOKB],
                                                   start=(hj == 0), stop=(hj == nKC - 1))
                            mm.then_inc(sem["TC"], 1)
                            it += 1
                C.inc("TC", nQ4 * nD)

                @block.vector
                def _(v):
                    bar_waits(v, bar)
                    it = 0
                    for q4 in range(nQ4):
                        for dj in range(nD):
                            s = it % 2
                            W(v, "TC", tc0 + it + 1)
                            if it >= 2:
                                W(v, "SC" + str(s), sc0[s] + 16 * ((it - 2) // 2 + 1))
                            for b in range(nBQ):
                                if scale_cols is not None:
                                    v.tensor_scalar(out=ob[s][:, b * TOKB:(b + 1) * TOKB],
                                                    in0=pp[s][b][:, :],
                                                    scalar1=scale_cols[:, dj:dj + 1],
                                                    scalar2=None, op0=ALU.mult)
                                else:
                                    v.tensor_copy(out=ob[s][:, b * TOKB:(b + 1) * TOKB],
                                                  in_=pp[s][b][:, :])
                            v.engine_nop().then_inc(sem["VC"], 1)
                            it += 1
                C.inc("VC", nQ4 * nD)

                @block.scalar
                def _(s_):
                    bar_waits(s_, bar)
                    it = 0
                    for q4 in range(nQ4):
                        for dj in range(nD):
                            s = it % 2
                            W(s_, "VC", vc0 + it + 1)
                            s_.dma_start(out=OUTD[dj * 128:(dj + 1) * 128, q4 * Q4:(q4 + 1) * Q4],
                                         in_=ob[s][:]).then_inc(sem["SC" + str(s)], 16)
                            it += 1
                C.inc("SC0", 16 * ((nQ4 * nD + 1) // 2))
                C.inc("SC1", 16 * ((nQ4 * nD) // 2))

        stage_blquant("c0", W1, T1N, nH, D, sw1)
        stage_transpose("c0b", T1N, T1T, H, D)

        barC1 = dict(C.v)
        with ExitStack() as sC1r:
            xq0T = [sC1r.enter_context(nc.sbuf_tensor(f"xq0T{i}", [128, TN], BF16))
                    for i in range(nD)]

            @block.sync
            def _(sync):
                bar_waits(sync, barC1)
                for di in range(nD):
                    sync.dma_start(out=xq0T[di][:], in_=XQ0[0:TN, di * 128:(di + 1) * 128],
                                   transpose=True).then_inc(sem["SYA"], 16)
            C.inc("SYA", 16 * nD)
            stage_featmm("c1", T1T, xq0T, H0T, nH, sw1)

        barC2 = dict(C.v)
        with ExitStack() as sC2:
            csb = lambda n, sh, dt=F32: sC2.enter_context(nc.sbuf_tensor(n, sh, dt))
            hgin = [csb(f"hgin{i}", [128, H], BF16) for i in range(2)]
            hg = [csb(f"hgC{i}", [128, H], BF16) for i in range(2)]
            qtmp = [csb(f"qtmpC{i}", [128, H]) for i in range(2)]
            xq2b = [csb(f"xq2b{i}", [128, H], BF16) for i in range(2)]
            cst = [csb(f"cstC2{i}", [128, 8]) for i in range(2)]
            ltC = csb("ltC2", [128, 1])
            ac0, gc0, vc0 = C.v["AC"], C.v["GC"], C.v["VC"]
            sy0 = [C.v["SY0"], C.v["SY1"]]
            sc0 = [C.v["SC0"], C.v["SC1"]]

            @block.sync
            def _(sync):
                bar_waits(sync, barC2)
                for t in range(nT):
                    b = t % 2
                    if t >= 2:
                        W(sync, "AC", ac0 + 2 * (t - 1))
                    sync.dma_start(out=hgin[b][:], in_=H0T[0:H, t * 128:(t + 1) * 128],
                                   transpose=True).then_inc(sem["SY" + str(b)], 16)
            C.inc("SY0", 16 * ((nT + 1) // 2))
            C.inc("SY1", 16 * (nT // 2))

            @block.scalar
            def _(s_):
                bar_waits(s_, barC2)
                for t in range(nT):
                    b = t % 2
                    W(s_, "SY" + str(b), sy0[b] + 16 * (t // 2 + 1))
                    if t >= 2:
                        W(s_, "GC", gc0 + t - 1)
                        W(s_, "VC", vc0 + 3 * (t - 1))
                    s_.activation(out=hg[b][:], in_=hgin[b][:], func=AF.Gelu_apprx_tanh,
                                  scale=rs0[:, t:t + 1]).then_inc(sem["AC"], 1)
                    W(s_, "VC", vc0 + 3 * t + 1)
                    VN(s_, s_.activation(out=ltC[:], in_=cst[b][:, 1:2], func=AF.Ln))
                    s_.activation(out=cst[b][:, 2:3], in_=ltC[:], func=AF.Exp,
                                  scale=-1.0).then_inc(sem["AC"], 1)
                    W(s_, "VC", vc0 + 3 * t + 3)
                    s_.dma_start(out=XQ2[t * 128:(t + 1) * 128, :], in_=xq2b[b][:]
                                 ).then_inc(sem["SC" + str(b)], 16)
            C.inc("AC", 2 * nT)
            C.inc("SC0", 16 * ((nT + 1) // 2))
            C.inc("SC1", 16 * (nT // 2))

            @block.gpsimd
            def _(g):
                bar_waits(g, barC2)
                for t in range(nT):
                    b = t % 2
                    W(g, "VC", vc0 + 3 * t + 2)
                    g.tensor_scalar(out=qtmp[b][:], in0=hg[b][:], scalar1=cst[b][:, 5:6],
                                    scalar2=MAGIC, op0=ALU.mult, op1=ALU.add
                                    ).then_inc(sem["GC"], 1)
            C.inc("GC", nT)

            @block.vector
            def _(v):
                bar_waits(v, barC2)
                for t in range(nT):
                    b = t % 2
                    W(v, "AC", ac0 + 2 * t + 1)
                    st = cst[b]
                    v.tensor_reduce(out=st[:, 0:1], in_=hg[b][:], axis=AX, op=ALU.max,
                                    apply_absolute_value=True).then_inc(sem["VSV"], 1)
                    v.wait_ge(sem["VSV"], C.inc("VSV"))
                    VN(v, v.tensor_scalar(out=st[:, 1:2], in0=st[:, 0:1], scalar1=1e-5,
                                          scalar2=None, op0=ALU.max))
                    v.tensor_scalar(out=rs2[:, t:t + 1], in0=st[:, 1:2], scalar1=1.0 / 127.0,
                                    scalar2=None, op0=ALU.mult).then_inc(sem["VC"], 1)
                    W(v, "AC", ac0 + 2 * t + 2)
                    VN(v, v.tensor_tensor(out=st[:, 3:4], in0=st[:, 2:3], in1=st[:, 1:2], op=ALU.mult))
                    VN(v, v.tensor_scalar(out=st[:, 3:4], in0=st[:, 3:4], scalar1=-1.0,
                                          scalar2=2.0, op0=ALU.mult, op1=ALU.add))
                    VN(v, v.tensor_tensor(out=st[:, 4:5], in0=st[:, 2:3], in1=st[:, 3:4], op=ALU.mult))
                    v.tensor_scalar(out=st[:, 5:6], in0=st[:, 4:5], scalar1=127.0,
                                    scalar2=None, op0=ALU.mult).then_inc(sem["VC"], 1)
                    W(v, "GC", gc0 + t + 1)
                    if t >= 2:
                        W(v, "SC" + str(b), sc0[b] + 16 * ((t - 2) // 2 + 1))
                    v.tensor_scalar(out=xq2b[b][:], in0=qtmp[b][:], scalar1=MAGIC,
                                    scalar2=None, op0=ALU.subtract).then_inc(sem["VC"], 1)
            C.inc("VC", 3 * nT)

        stage_blquant("c2w", W2, T2N, nD, H, sw2)
        stage_transpose("c3b", T2N, T2T, D, H)
        stage_qmm("c3", XQ2, T2T, OF0T, nH, sw2)

        barD = dict(C.v)
        with ExitStack() as sD:
            dsb = lambda n, sh, dt=F32: sD.enter_context(nc.sbuf_tensor(n, sh, dt))
            xt = [dsb(f"xtD{i}", [128, D]) for i in range(2)]
            s0t = [dsb(f"s0tD{i}", [128, D], BF16) for i in range(2)]
            ofT = [dsb(f"ofTD{i}", [128, D], BF16) for i in range(2)]
            g1b = [dsb(f"g1bD{i}", [128, D], BF16) for i in range(2)]
            sgD = dsb("sgD", [128, D])
            x1b = [dsb(f"x1bD{i}", [128, D]) for i in range(2)]
            xn1b = [dsb(f"xn1bD{i}", [128, D], BF16) for i in range(2)]
            sqjD = [dsb(f"sqjD{i}", [128, D]) for i in range(2)]
            dstT = [dsb(f"dstD{i}", [128, 16]) for i in range(2)]
            ltD = dsb("ltD", [128, 1])
            ac0, vc0 = C.v["AC"], C.v["VC"]
            sy0 = [C.v["SY0"], C.v["SY1"]]
            gpD0 = [C.v["GP0"], C.v["GP1"]]
            sc0 = [C.v["SC0"], C.v["SC1"]]

            @block.sync
            def _(sync):
                bar_waits(sync, barD)
                for t in range(nT):
                    b = t % 2
                    if t >= 2:
                        W(sync, "VC", vc0 + 3 * (t - 1))
                        W(sync, "AC", ac0 + 3 * (t - 1))
                    sync.dma_start(out=xt[b][:], in_=X[t * 128:(t + 1) * 128, :]).then_inc(sem["SY" + str(b)], 16)
                    sync.dma_start(out=ofT[b][:], in_=OF0T[0:D, t * 128:(t + 1) * 128],
                                   transpose=True).then_inc(sem["SY" + str(b)], 16)
            C.inc("SY0", 32 * ((nT + 1) // 2))
            C.inc("SY1", 32 * (nT // 2))

            @block.gpsimd
            def _(g):
                bar_waits(g, barD)
                for t in range(nT):
                    b = t % 2
                    if t >= 2:
                        W(g, "VC", vc0 + 3 * (t - 1))
                    g.dma_start(out=s0t[b][:], in_=S0[t * 128:(t + 1) * 128, :]).then_inc(sem["GP" + str(b)], 16)
            C.inc("GP0", 16 * ((nT + 1) // 2))
            C.inc("GP1", 16 * (nT // 2))

            @block.scalar
            def _(s_):
                bar_waits(s_, barD)
                for t in range(nT):
                    b = t % 2
                    W(s_, "SY" + str(b), sy0[b] + 32 * (t // 2 + 1))
                    s_.activation(out=g1b[b][:], in_=ofT[b][:], func=AF.Silu,
                                  scale=rs2[:, t:t + 1]).then_inc(sem["AC"], 1)
                    W(s_, "VC", vc0 + 3 * t + 1)
                    s_.activation(out=sqjD[b][:], in_=x1b[b][:], func=AF.Square
                                  ).then_inc(sem["AC"], 1)
                    W(s_, "VC", vc0 + 3 * t + 2)
                    VN(s_, s_.activation(out=ltD[:], in_=dstT[b][:, 6:7], func=AF.Ln))
                    s_.activation(out=dstT[b][:, 7:8], in_=ltD[:], func=AF.Exp,
                                  scale=-0.5).then_inc(sem["AC"], 1)
                    W(s_, "VC", vc0 + 3 * t + 3)
                    s_.dma_start(out=X1[t * 128:(t + 1) * 128, :], in_=x1b[b][:]).then_inc(sem["SC" + str(b)], 16)
                    s_.dma_start(out=XN1[t * 128:(t + 1) * 128, :], in_=xn1b[b][:]).then_inc(sem["SC" + str(b)], 16)
            C.inc("AC", 3 * nT)
            C.inc("SC0", 32 * ((nT + 1) // 2))
            C.inc("SC1", 32 * (nT // 2))

            @block.vector
            def _(v):
                bar_waits(v, barD)
                for t in range(nT):
                    b = t % 2
                    if t >= 2:
                        W(v, "SC" + str(b), sc0[b] + 32 * ((t - 2) // 2 + 1))
                    W(v, "AC", ac0 + 3 * t + 1)
                    W(v, "GP" + str(b), gpD0[b] + 16 * (t // 2 + 1))
                    st = dstT[b]
                    VN(v, v.tensor_tensor(out=sgD[:], in0=g1b[b][:], in1=s0t[b][:], op=ALU.mult))
                    VN(v, v.tensor_tensor(out=x1b[b][:], in0=sgD[:], in1=xt[b][:], op=ALU.add))
                    v.reduce_sum(out=st[:, 0:1], in_=x1b[b][:], axis=AX
                                 ).then_inc(sem["VSV"], 1)
                    v.wait_ge(sem["VSV"], C.inc("VSV"))
                    v.engine_nop().then_inc(sem["VC"], 1)
                    W(v, "AC", ac0 + 3 * t + 2)
                    v.reduce_sum(out=st[:, 1:2], in_=sqjD[b][:], axis=AX
                                 ).then_inc(sem["VSV"], 1)
                    v.wait_ge(sem["VSV"], C.inc("VSV"))
                    VN(v, v.tensor_scalar(out=st[:, 2:3], in0=st[:, 0:1], scalar1=1.0 / D,
                                          scalar2=None, op0=ALU.mult))
                    VN(v, v.tensor_scalar(out=st[:, 3:4], in0=st[:, 1:2], scalar1=1.0 / D,
                                          scalar2=None, op0=ALU.mult))
                    VN(v, v.tensor_tensor(out=st[:, 4:5], in0=st[:, 2:3], in1=st[:, 2:3], op=ALU.mult))
                    VN(v, v.tensor_tensor(out=st[:, 5:6], in0=st[:, 3:4], in1=st[:, 4:5], op=ALU.subtract))
                    v.tensor_scalar(out=st[:, 6:7], in0=st[:, 5:6], scalar1=LN_EPS,
                                    scalar2=None, op0=ALU.add).then_inc(sem["VC"], 1)
                    W(v, "AC", ac0 + 3 * t + 3)
                    VN(v, v.tensor_tensor(out=st[:, 8:9], in0=st[:, 7:8], in1=st[:, 7:8], op=ALU.mult))
                    VN(v, v.tensor_tensor(out=st[:, 9:10], in0=st[:, 8:9], in1=st[:, 6:7], op=ALU.mult))
                    VN(v, v.tensor_scalar(out=st[:, 9:10], in0=st[:, 9:10], scalar1=-0.5,
                                          scalar2=1.5, op0=ALU.mult, op1=ALU.add))
                    VN(v, v.tensor_tensor(out=st[:, 10:11], in0=st[:, 7:8], in1=st[:, 9:10], op=ALU.mult))
                    v.tensor_scalar(out=xn1b[b][:], in0=x1b[b][:], scalar1=st[:, 2:3],
                                    scalar2=st[:, 10:11], op0=ALU.subtract, op1=ALU.mult
                                    ).then_inc(sem["VC"], 1)
            C.inc("VC", 3 * nT)

        barE0 = dict(C.v)
        with ExitStack() as sE0:
            esb = lambda n, sh, dt=F32: sE0.enter_context(nc.sbuf_tensor(n, sh, dt))
            vsb = [esb(f"vsb{t}", [128, G * (HD + 1)], BF16) for t in range(nT)]
            vbiasB = esb("vbiasB", [128, KV])
            vrow = esb("vrowE", [1, KV])

            with ExitStack() as sE0i:
                esbi = lambda n, sh, dt=F32: sE0i.enter_context(nc.sbuf_tensor(n, sh, dt))
                xn1T = [esbi(f"xn1T{i}", [128, TN], BF16) for i in range(nD)]
                wbkE = [esbi(f"wbkE{i}", [128, nD * 128], BF16) for i in range(2)]
                wv = [esbi(f"wvE{i}", [128, KV], BF16) for i in range(nD)]
                qkb = [esbi(f"qkbE{i}", [128, Q4], BF16) for i in range(2)]
                pq = [[sE0i.enter_context(nc.psum_tensor(f"pqE{s}{b}", [128, TOKB], F32))
                       for b in range(nBQ)] for s in range(2)]
                pv = [sE0i.enter_context(nc.psum_tensor(f"pvE{s}", [128, KV], F32))
                      for s in range(2)]
                pvb = sE0i.enter_context(nc.psum_tensor("pvbE", [128, KV], F32))

                @block.sync
                def _(sync):
                    bar_waits(sync, barE0)
                    for di in range(nD):
                        sync.dma_start(out=xn1T[di][:], in_=XN1[0:TN, di * 128:(di + 1) * 128],
                                       transpose=True).then_inc(sem["SYA"], 16)
                    sync.dma_start(out=vrow[:], in_=BQKV[D + KV:D + 2 * KV].unsqueeze(0)
                                   ).then_inc(sem["SYA"], 16)
                C.inc("SYA", 16 * (nD + 1))
                sy_res = C.v["SYA"]

                @block.gpsimd
                def _(g):
                    bar_waits(g, barE0)
                    for di in range(nD):
                        g.dma_start(out=wv[di][:],
                                    in_=WQKV[nQK:nQKV, di].transpose([1, 0, 2])
                                    ).then_inc(sem["GPA"], 16)
                C.inc("GPA", 16 * nD)
                gp_wv = C.v["GPA"]

                tcE0a = C.v["TC"]

                @block.tensor
                def _(t_):
                    bar_waits(t_, barE0)
                    W(t_, "SYA", sy_res)
                    nvb = max(1, KV // NMM)
                    wv_ = KV // nvb
                    li = None
                    for i in range(nvb):
                        li = t_.matmul(out=pvb[:, i * wv_:(i + 1) * wv_], lhsT=onesF_1x128[:],
                                       rhs=vrow[:, i * wv_:(i + 1) * wv_], start=True, stop=True)
                    li.then_inc(sem["TC"], 1)
                C.inc("TC", 1)
                tcE0b = C.v["TC"]

                @block.vector
                def _(v):
                    bar_waits(v, barE0)
                    W(v, "TC", tcE0b)
                    v.tensor_copy(out=vbiasB[:], in_=pvb[:, :]).then_inc(sem["VC"], 1)
                C.inc("VC")
                vc_vb = C.v["VC"]

                gpqk0 = [C.v["GP0"], C.v["GP1"]]
                tcqk0 = C.v["TC"]
                scqk0 = [C.v["SC0"], C.v["SC1"]]

                @block.gpsimd
                def _(g):
                    for ej in range(nQK):
                        if ej >= 2:
                            W(g, "TC", tcqk0 + (ej - 1) * nQ4)
                        g.dma_start(out=wbkE[ej % 2][:], in_=WQKV[ej].transpose([1, 0, 2])
                                    ).then_inc(sem["GP" + str(ej % 2)], 16)
                C.inc("GP0", 16 * ((nQK + 1) // 2))
                C.inc("GP1", 16 * (nQK // 2))

                @block.tensor
                def _(t_):
                    it = 0
                    for ej in range(nQK):
                        W(t_, "GP" + str(ej % 2), gpqk0[ej % 2] + 16 * (ej // 2 + 1))
                        for q4 in range(nQ4):
                            s = it % 2
                            if it >= 2:
                                W(t_, "VC", vc_vb + (it - 1))
                            mm = None
                            for b in range(nBQ):
                                off = q4 * Q4 + b * TOKB
                                for di in range(nD):
                                    mm = t_.matmul(out=pq[s][b][:, :],
                                                   lhsT=wbkE[ej % 2][:, di * 128:(di + 1) * 128],
                                                   rhs=xn1T[di][:, off:off + TOKB],
                                                   start=(di == 0), stop=(di == nD - 1))
                            mm.then_inc(sem["TC"], 1)
                            it += 1
                C.inc("TC", nQK * nQ4)

                @block.vector
                def _(v):
                    it = 0
                    for ej in range(nQK):
                        for q4 in range(nQ4):
                            s = it % 2
                            W(v, "TC", tcqk0 + it + 1)
                            if it >= 2:
                                W(v, "SC" + str(s), scqk0[s] + 16 * ((it - 2) // 2 + 1))
                            for b in range(nBQ):
                                if ej < nD:
                                    v.tensor_scalar(out=qkb[s][:, b * TOKB:(b + 1) * TOKB],
                                                    in0=pq[s][b][:, :],
                                                    scalar1=bqkv_cols[:, ej:ej + 1],
                                                    scalar2=0.125, op0=ALU.add, op1=ALU.mult)
                                else:
                                    v.tensor_scalar(out=qkb[s][:, b * TOKB:(b + 1) * TOKB],
                                                    in0=pq[s][b][:, :],
                                                    scalar1=bqkv_cols[:, ej:ej + 1],
                                                    scalar2=None, op0=ALU.add)
                            v.engine_nop().then_inc(sem["VC"], 1)
                            it += 1
                C.inc("VC", nQK * nQ4)

                @block.scalar
                def _(s_):
                    bar_waits(s_, barE0)
                    it = 0
                    for ej in range(nQK):
                        for q4 in range(nQ4):
                            s = it % 2
                            W(s_, "VC", vc_vb + it + 1)
                            if ej < nD:
                                s_.dma_start(out=QT[ej * 128:(ej + 1) * 128, q4 * Q4:(q4 + 1) * Q4],
                                             in_=qkb[s][:]).then_inc(sem["SC" + str(s)], 16)
                            else:
                                e2 = ej - nD
                                s_.dma_start(out=KT[e2 * 128:(e2 + 1) * 128, q4 * Q4:(q4 + 1) * Q4],
                                             in_=qkb[s][:]).then_inc(sem["SC" + str(s)], 16)
                            it += 1
                C.inc("SC0", 16 * ((nQK * nQ4 + 1) // 2))
                C.inc("SC1", 16 * ((nQK * nQ4) // 2))

                tcv0 = C.v["TC"]
                vcv0 = C.v["VC"]

                @block.tensor
                def _(t_):
                    W(t_, "GPA", gp_wv)
                    for t in range(nT):
                        s = t % 2
                        if t >= 2:
                            W(t_, "VC", vcv0 + t - 1)
                        mm = None
                        for di in range(nD):
                            mm = t_.matmul(out=pv[s][:, :], lhsT=xn1T[di][:, t * 128:(t + 1) * 128],
                                           rhs=wv[di][:], start=(di == 0), stop=(di == nD - 1))
                        mm.then_inc(sem["TC"], 1)
                C.inc("TC", nT)

                @block.vector
                def _(v):
                    for t in range(nT):
                        s = t % 2
                        v.memset(vsb[t][:].rearrange("p (g c) -> p g c", c=HD + 1)[:, :, HD:HD + 1],
                                 1.0)
                        W(v, "TC", tcv0 + t + 1)
                        v.tensor_tensor(out=vsb[t][:].rearrange("p (g c) -> p g c", c=HD + 1)[:, :, 0:HD],
                                        in0=pv[s][:, :].rearrange("p (g c) -> p g c", c=HD),
                                        in1=vbiasB[:].rearrange("p (g c) -> p g c", c=HD),
                                        op=ALU.add).then_inc(sem["VC"], 1)
                C.inc("VC", nT)

            barE1 = dict(C.v)
            with ExitStack() as sE1:
                asb1 = lambda n, sh, dt=F32: sE1.enter_context(nc.sbuf_tensor(n, sh, dt))
                KSB = [asb1(f"KSB{g}", [64, S], BF16) for g in range(G)]
                QH = [asb1(f"QH{i}", [64, S], BF16) for i in range(2)]
                qpre = [asb1(f"qpre{i}", [64, S], BF16) for i in range(2)]
                ksqP = [asb1(f"ksqP{i}", [64, S], BF16) for i in range(2)]
                attnT = [[asb1(f"attnT{a}{k}", [128, QB], BF16) for k in range(KC)]
                         for a in range(2)]
                nst = asb1("nstE1", [1, 16])
                kst = asb1("kstE1", [1, G])
                negbAll = asb1("negbAll", [128, NSEQ * G * R])
                otb = asb1("otbE1", [64, QB], BF16)
                lrowf = asb1("lrowfE1", [1, QB])
                NPS = min(4, KC)
                ps = [sE1.enter_context(nc.psum_tensor(f"psE1{i}", [128, QB], F32))
                      for i in range(NPS)]
                pav = [sE1.enter_context(nc.psum_tensor(f"pavE1{i}", [65, QB], F32))
                       for i in range(2)]
                pnP = [sE1.enter_context(nc.psum_tensor("pnE1", [1, NMM], F32))]
                pbn = sE1.enter_context(nc.psum_tensor("pbnE1", [128, 1], F32))

                nNB = S // NMM
                HEADS = G * R
                tc_last_scores = [0, 0]
                tc_seq_scores = 0
                gp_last_ot = None
                pav_vc_guard = [0, 0]
                vc_negb = {}
                pn_cnt = 0
                pn_guard = [0]
                tpn_last = [0, 0]
                ac_sqrt_last = [0, 0]
                ac_kst = {}
                vc_negb_prev = 0
                qpre_guard = [0, 0]

                @block.sync
                def _(sync):
                    bar_waits(sync, barE1)

                @block.vector
                def _(v):
                    bar_waits(v, barE1)

                @block.scalar
                def _(s_):
                    bar_waits(s_, barE1)

                @block.tensor
                def _(t_):
                    bar_waits(t_, barE1)

                @block.gpsimd
                def _(g_):
                    bar_waits(g_, barE1)

                ci = 0
                for seq in range(NSEQ):
                    tc_seq_prev = tc_seq_scores

                    @block.sync
                    def _(sync, tc_seq_prev=tc_seq_prev):
                        W(sync, "TC", tc_seq_prev)
                        for g in range(G):
                            sync.dma_start(out=KSB[g][:],
                                           in_=KT[0:KV, seq * S:(seq + 1) * S]
                                           .rearrange("(g c) s -> g c s", c=64)[g]
                                           ).then_inc(sem["SYA"], 16)
                    C.inc("SYA", 16 * G)
                    sy_ks1 = C.v["SYA"]

                    def norm_core(ci, src, sy_need, sy_sem):
                        cp = ci % 2
                        nb = 8 * cp
                        tpn_prev = tpn_last[cp]
                        ac_prev = ac_sqrt_last[cp]

                        @block.vector
                        def _(v, cp=cp, src=src, sy_need=sy_need, sy_sem=sy_sem,
                              tpn_prev=tpn_prev, ac_prev=ac_prev):
                            W(v, sy_sem, sy_need)
                            W(v, "TC", tpn_prev)
                            W(v, "AC", ac_prev)
                            v.tensor_tensor(out=ksqP[cp][:], in0=src[:], in1=src[:],
                                            op=ALU.mult).then_inc(sem["VC"], 1)
                        C.inc("VC")
                        vc_ksq = C.v["VC"]
                        for qb in range(nNB):
                            pp = 0
                            pn_cnt_box[0] += 1
                            png = pn_guard[pp]

                            @block.tensor
                            def _(t_, cp=cp, qb=qb, pp=pp, vc_ksq=vc_ksq, png=png):
                                W(t_, "VC", vc_ksq)
                                W(t_, "VC", png)
                                t_.matmul(out=pnP[0][:, :], lhsT=onesB_64x1[:],
                                          rhs=ksqP[cp][:, qb * NMM:(qb + 1) * NMM],
                                          start=True, stop=True).then_inc(sem["TC"], 1)
                            C.inc("TC")
                            tcq = C.v["TC"]

                            @block.vector
                            def _(v, cp=cp, qb=qb, pp=pp, tcq=tcq):
                                W(v, "TC", tcq)
                                v.tensor_reduce(out=nst[:, 8 * cp + qb:8 * cp + qb + 1],
                                                in_=pnP[0][:, :], axis=AX,
                                                op=ALU.max).then_inc(sem["VC"], 1)
                            C.inc("VC")
                            pn_guard[pp] = C.v["VC"]
                        tpn_last[cp] = C.v["TC"]
                        vc_maxes = C.v["VC"]

                        @block.vector
                        def _(v, cp=cp, vc_maxes=vc_maxes):
                            W(v, "VC", vc_maxes)
                            v.tensor_reduce(out=nst[:, 8 * cp + 4:8 * cp + 5],
                                            in_=nst[:, 8 * cp:8 * cp + nNB], axis=AX,
                                            op=ALU.max).then_inc(sem["VC"], 1)
                        C.inc("VC")
                        return cp, C.v["VC"]

                    pn_cnt_box = [pn_cnt]
                    for g in range(G):
                        cp, vc_red = norm_core(ci, KSB[g], sy_ks1, "SYA")
                        ci += 1

                        @block.scalar
                        def _(s_, g=g, cp=cp, vc_red=vc_red):
                            W(s_, "VC", vc_red)
                            s_.activation(out=kst[:, g:g + 1], in_=nst[:, 8 * cp + 4:8 * cp + 5],
                                          func=AF.Sqrt).then_inc(sem["AC"], 1)
                        C.inc("AC")
                        ac_kst[(seq, g)] = C.v["AC"]
                        ac_sqrt_last[cp] = C.v["AC"]

                        for r in range(R):
                            head = g * R + r
                            cp = ci % 2
                            qg = qpre_guard[cp]

                            @block.sync
                            def _(sync, head=head, seq=seq, cp=cp, qg=qg):
                                W(sync, "TC", qg)
                                sync.dma_start(out=qpre[cp][:],
                                               in_=QT[head * 64:(head + 1) * 64,
                                                      seq * S:(seq + 1) * S]
                                               ).then_inc(sem["SY" + str(2 + cp)], 16)
                            C.inc("SY" + str(2 + cp), 16)
                            sy_qp = C.v["SY" + str(2 + cp)]

                            cp2, vc_red = norm_core(ci, qpre[cp], sy_qp, "SY" + str(2 + cp))
                            ci += 1
                            qpre_guard[cp] = tpn_last[cp2]

                            @block.scalar
                            def _(s_, cp=cp2, vc_red=vc_red):
                                W(s_, "VC", vc_red)
                                s_.activation(out=nst[:, 8 * cp + 5:8 * cp + 6],
                                              in_=nst[:, 8 * cp + 4:8 * cp + 5],
                                              func=AF.Sqrt).then_inc(sem["AC"], 1)
                            C.inc("AC")
                            ac_q = C.v["AC"]
                            ac_sqrt_last[cp2] = C.v["AC"]

                            @block.vector
                            def _(v, g=g, cp=cp2, ac_q=ac_q, ack=ac_kst[(seq, g)]):
                                W(v, "AC", ac_q)
                                W(v, "AC", ack)
                                VN(v, v.tensor_tensor(out=nst[:, 8 * cp + 6:8 * cp + 7],
                                                      in0=nst[:, 8 * cp + 5:8 * cp + 6],
                                                      in1=kst[:, g:g + 1], op=ALU.mult))
                                v.tensor_scalar(out=nst[:, 8 * cp + 7:8 * cp + 8],
                                                in0=nst[:, 8 * cp + 6:8 * cp + 7],
                                                scalar1=-1.0, scalar2=None,
                                                op0=ALU.mult).then_inc(sem["VC"], 1)
                            C.inc("VC")
                            vc_bnd = C.v["VC"]

                            @block.tensor
                            def _(t_, cp=cp2, vc_bnd=vc_bnd, vprev=vc_negb_prev):
                                W(t_, "VC", vc_bnd)
                                W(t_, "VC", vprev)
                                t_.matmul(out=pbn[:, :], lhsT=onesF_1x128[:],
                                          rhs=nst[:, 8 * cp + 7:8 * cp + 8],
                                          start=True, stop=True).then_inc(sem["TC"], 1)
                            C.inc("TC")
                            tc_nb = C.v["TC"]

                            @block.vector
                            def _(v, col=seq * HEADS + head, tc_nb=tc_nb):
                                W(v, "TC", tc_nb)
                                v.tensor_copy(out=negbAll[:, col:col + 1],
                                              in_=pbn[:, :]).then_inc(sem["VC"], 1)
                            C.inc("VC")
                            vc_negb[(seq, head)] = C.v["VC"]
                            vc_negb_prev = C.v["VC"]
                    pn_cnt = pn_cnt_box[0]

                    for g in range(G):
                        for r in range(R):
                            head = g * R + r
                            hb = head % 2
                            tc_guard = tc_last_scores[hb]

                            @block.sync
                            def _(sync, head=head, seq=seq, hb=hb, tc_guard=tc_guard):
                                W(sync, "TC", tc_guard)
                                sync.dma_start(out=QH[hb][:],
                                               in_=QT[head * 64:(head + 1) * 64,
                                                      seq * S:(seq + 1) * S]
                                               ).then_inc(sem["SY" + str(hb)], 16)
                            C.inc("SY" + str(hb), 16)
                            sy_qh1 = C.v["SY" + str(hb)]

                            for qblk in range(nQB):
                                a = (head * nQB + qblk) % 2
                                q0 = qblk * QB
                                tc_s0 = C.v["TC"]
                                ac_e0 = C.v["AC"]

                                pav_g = pav_vc_guard[a]
                                vcn = vc_negb[(seq, head)]

                                @block.tensor
                                def _(t_, g=g, hb=hb, a=a, q0=q0, seq=seq, qblk=qblk,
                                      ac_e0=ac_e0, vcn=vcn, pav_g=pav_g, sy_qh1=sy_qh1):
                                    if qblk == 0:
                                        W(t_, "SYA", sy_ks1)
                                        W(t_, "SY" + str(hb), sy_qh1)
                                        W(t_, "VC", vcn)
                                    W(t_, "VC", pav_g)
                                    for kc in range(KC):
                                        if kc >= NPS and kc % NPS == 0:
                                            W(t_, "AC", ac_e0 + kc)
                                        t_.matmul(out=ps[kc % NPS][:, :],
                                                  lhsT=KSB[g][:, kc * 128:(kc + 1) * 128],
                                                  rhs=QH[hb][:, q0:q0 + QB],
                                                  start=True, stop=True).then_inc(sem["TC"], 1)
                                    mm = None
                                    for kc in range(KC):
                                        if kc % NPS == 0:
                                            W(t_, "AC", ac_e0 + min(kc + NPS, KC))
                                        mm = t_.matmul(out=pav[a][:, :],
                                                       lhsT=vsb[seq * nTS + kc][:,
                                                            g * (HD + 1):(g + 1) * (HD + 1)],
                                                       rhs=attnT[a][kc][:],
                                                       start=(kc == 0), stop=(kc == KC - 1))
                                    mm.then_inc(sem["TC"], 1)
                                C.inc("TC", KC + 1)
                                tc_avdone = C.v["TC"]
                                tc_last_scores[hb] = tc_avdone
                                tc_seq_scores = tc_avdone

                                @block.scalar
                                def _(s_, a=a, tc_s0=tc_s0, col=seq * HEADS + head):
                                    for kc in range(KC):
                                        if kc % NPS == 0:
                                            W(s_, "TC", tc_s0 + min(kc + NPS, KC))
                                        s_.activation(out=attnT[a][kc][:], in_=ps[kc % NPS][:, :],
                                                      func=AF.Exp, bias=negbAll[:, col:col + 1]
                                                      ).then_inc(sem["AC"], 1)
                                C.inc("AC", KC)

                                gp_ot_prev = gp_last_ot

                                @block.vector
                                def _(v, a=a, tc_avdone=tc_avdone, gp_ot_prev=gp_ot_prev):
                                    W(v, "TC", tc_avdone)
                                    if gp_ot_prev is not None:
                                        W(v, "GP" + str(gp_ot_prev[0]), gp_ot_prev[1])
                                    v.tensor_copy(out=otb[:], in_=pav[a][0:64, :])
                                    v.tensor_copy(out=lrowf[:], in_=pav[a][64:65, :]
                                                  ).then_inc(sem["VC"], 1)
                                C.inc("VC")
                                vc_ot = C.v["VC"]
                                pav_vc_guard[a] = vc_ot

                                @block.gpsimd
                                def _(g_, head=head, seq=seq, q0=q0, vc_ot=vc_ot):
                                    W(g_, "VC", vc_ot)
                                    tj0 = (seq * S + q0) // 128
                                    ns = QB // 128
                                    ei = head // 2
                                    ro = 64 * (head % 2)
                                    g_.dma_start(out=OTU[ei, tj0:tj0 + ns, ro:ro + 64, :]
                                                 .transpose([1, 0, 2]),
                                                 in_=otb[:]).then_inc(sem["GP" + str(a)], 16)
                                    g_.dma_start(out=LROWS[seq * G * R + head, q0:q0 + QB]
                                                 .unsqueeze(0),
                                                 in_=lrowf[:]).then_inc(sem["GP" + str(a)], 16)
                                C.inc("GP" + str(a), 32)
                                gp_last_ot = (a, C.v["GP" + str(a)])

        barE1b = dict(C.v)
        I32 = mybir.dt.int32
        HEADS_ALL = NSEQ * G * R
        with ExitStack() as sEb:
            ebsb = lambda n, sh, dt=F32: sEb.enter_context(nc.sbuf_tensor(n, sh, dt))
            lb = ebsb("lbE1b", [HEADS_ALL, S])
            padj = ebsb("padjE1b", [HEADS_ALL, S], I32)
            lhat = ebsb("lhatE1b", [HEADS_ALL, S])
            lnh = ebsb("lnhE1b", [HEADS_ALL, S])
            rech = ebsb("rechE1b", [HEADS_ALL, S])
            recf = ebsb("recfE1b", [HEADS_ALL, S])
            @block.sync
            def _(sync):
                bar_waits(sync, barE1b)
                sync.dma_start(out=lb[:], in_=LROWS).then_inc(sem["SYA"], 16)
            C.inc("SYA", 16)
            syE1b = C.v["SYA"]

            @block.vector
            def _(v):
                bar_waits(v, barE1b)
                W(v, "SYA", syE1b)
                v.tensor_scalar(out=padj[:], in0=lb[:].bitcast(I32), scalar1=0x7F800000,
                                scalar2=None, op0=ALU.bitwise_and).then_inc(sem["VSV"], 1)
                v.wait_ge(sem["VSV"], C.inc("VSV"))
                v.tensor_scalar(out=padj[:], in0=padj[:], scalar1=0x3F800000,
                                scalar2=None, op0=ALU.subtract).then_inc(sem["VSV"], 1)
                v.wait_ge(sem["VSV"], C.inc("VSV"))
                v.tensor_tensor(out=lhat[:].bitcast(I32), in0=lb[:].bitcast(I32),
                                in1=padj[:], op=ALU.subtract).then_inc(sem["VC"], 1)
            C.inc("VC")
            vcE1b = C.v["VC"]

            @block.scalar
            def _(s_):
                bar_waits(s_, barE1b)
                W(s_, "VC", vcE1b)
                VN(s_, s_.activation(out=lnh[:], in_=lhat[:], func=AF.Ln))
                s_.activation(out=rech[:], in_=lnh[:], func=AF.Exp, scale=-1.0
                              ).then_inc(sem["AC"], 1)
            C.inc("AC")
            acE1b = C.v["AC"]

            @block.vector
            def _(v):
                W(v, "AC", acE1b)
                v.tensor_tensor(out=recf[:].bitcast(I32), in0=rech[:].bitcast(I32),
                                in1=padj[:], op=ALU.subtract).then_inc(sem["VC"], 1)
            C.inc("VC")
            vcE1b2 = C.v["VC"]

            @block.scalar
            def _(s_):
                W(s_, "VC", vcE1b2)
                s_.dma_start(out=RBULK, in_=recf[:]).then_inc(sem["SCA"], 16)
            C.inc("SCA", 16)

        barE1c = dict(C.v)
        with ExitStack() as sEc:
            ecsb = lambda n, sh, dt=F32: sEc.enter_context(nc.sbuf_tensor(n, sh, dt))
            otu = [ecsb(f"otuEc{i}", [128, QB], BF16) for i in range(2)]
            rec2 = [ecsb(f"rec2Ec{i}", [2, QB]) for i in range(2)]
            sel2 = ecsb("sel2Ec", [2, 128])
            otn = [ecsb(f"otnEc{i}", [128, QB], BF16) for i in range(2)]
            pb2 = [sEc.enter_context(nc.psum_tensor(f"pb2Ec{i}", [128, QB], F32))
                   for i in range(2)]
            tc0, vc0, ac0 = C.v["TC"], C.v["VC"], C.v["AC"]
            sy0 = [C.v["SY0"], C.v["SY1"]]
            sc0 = [C.v["SC0"], C.v["SC1"]]
            niterC = NSEQ * nD * nQB

            @block.vector
            def _(v):
                bar_waits(v, barE1c)
                VN(v, v.memset(sel2[:], 0.0))
                v.memset(sel2[0:1, 0:64], 1.0).then_inc(sem["VC"], 1)
            C.inc("VC")
            vc_sel0 = C.v["VC"]

            @block.gpsimd
            def _(g):
                bar_waits(g, barE1c)
                W(g, "VC", vc_sel0)
                g.dma_start(out=sel2[1:2, 64:128], in_=sel2[0:1, 0:64]).then_inc(sem["GPA"], 16)
            C.inc("GPA", 16)
            gp_sel = C.v["GPA"]

            @block.vector
            def _(v):
                W(v, "GPA", gp_sel)
                v.engine_nop().then_inc(sem["VC"], 1)
            C.inc("VC")
            vc_sel = C.v["VC"]

            @block.sync
            def _(sync):
                bar_waits(sync, barE1c)
                it = 0
                for seq in range(NSEQ):
                    for ei in range(nD):
                        for qblk in range(nQB):
                            b = it % 2
                            if it >= 2:
                                W(sync, "TC", tc0 + it - 1)
                                W(sync, "VC", vc_sel + it - 1)
                            tj0 = (seq * S + qblk * QB) // 128
                            ns = QB // 128
                            sync.dma_start(out=otu[b][:],
                                           in_=OTU[ei, tj0:tj0 + ns].transpose([1, 0, 2])
                                           ).then_inc(sem["SY" + str(b)], 16)
                            sync.dma_start(out=rec2[b][:],
                                           in_=RBULK[seq * G * R + 2 * ei:seq * G * R + 2 * ei + 2,
                                                     qblk * QB:(qblk + 1) * QB]
                                           ).then_inc(sem["SY" + str(b)], 16)
                            it += 1
            C.inc("SY0", 32 * ((niterC + 1) // 2))
            C.inc("SY1", 32 * (niterC // 2))

            @block.tensor
            def _(t_):
                bar_waits(t_, barE1c)
                W(t_, "VC", vc_sel)
                for it in range(niterC):
                    b = it % 2
                    W(t_, "SY" + str(b), sy0[b] + 32 * (it // 2 + 1))
                    if it >= 2:
                        W(t_, "VC", vc_sel + it - 1)
                    t_.matmul(out=pb2[b][:, :], lhsT=sel2[:], rhs=rec2[b][:],
                              start=True, stop=True).then_inc(sem["TC"], 1)
            C.inc("TC", niterC)

            @block.vector
            def _(v):
                for it in range(niterC):
                    b = it % 2
                    W(v, "TC", tc0 + it + 1)
                    if it >= 2:
                        W(v, "SC" + str(b), sc0[b] + 16 * ((it - 2) // 2 + 1))
                    v.tensor_tensor(out=otn[b][:], in0=pb2[b][:, :], in1=otu[b][:],
                                    op=ALU.mult).then_inc(sem["VC"], 1)
            C.inc("VC", niterC)

            @block.scalar
            def _(s_):
                bar_waits(s_, barE1c)
                it = 0
                for seq in range(NSEQ):
                    for ei in range(nD):
                        for qblk in range(nQB):
                            b = it % 2
                            W(s_, "VC", vc_sel + it + 1)
                            tj0 = (seq * S + qblk * QB) // 128
                            ns = QB // 128
                            s_.dma_start(out=OTT[ei, tj0:tj0 + ns].transpose([1, 0, 2]),
                                         in_=otn[b][:]).then_inc(sem["SC" + str(b)], 16)
                            it += 1
            C.inc("SC0", 16 * ((niterC + 1) // 2))
            C.inc("SC1", 16 * (niterC // 2))

        barE2 = dict(C.v)
        with ExitStack() as sE2:
            esb2 = lambda n, sh, dt=F32: sE2.enter_context(nc.sbuf_tensor(n, sh, dt))
            wstr1 = [esb2(f"wstr1_{i}", [128, D], BF16) for i in range(nD)]
            lblk1 = [esb2(f"lblkE2{i}", [128, nD * 128], BF16) for i in range(2)]
            s1b = [esb2(f"s1bE2{i}", [128, D], BF16) for i in range(2)]
            borow = esb2("borow", [1, D])
            boutB = esb2("boutB", [128, D])
            pd1 = [[sE2.enter_context(nc.psum_tensor(f"pdE2{s}{b}", [128, NMM], F32))
                    for b in range(nDB)] for s in range(2)]

            @block.gpsimd
            def _(g):
                bar_waits(g, barE2)
                for ei in range(nD):
                    g.dma_start(out=wstr1[ei][:], in_=WOUT1[ei]).then_inc(sem["GPA"], 16)
            C.inc("GPA", 16 * nD)
            gpE2 = C.v["GPA"]

            @block.sync
            def _(sync):
                bar_waits(sync, barE2)
                sync.dma_start(out=borow[:], in_=BOUT1.unsqueeze(0)).then_inc(sem["SYA"], 16)
            C.inc("SYA", 16)
            syE2b = C.v["SYA"]
            tcE2a = C.v["TC"]
            vcE2a = C.v["VC"]

            @block.tensor
            def _(t_):
                bar_waits(t_, barE2)
                W(t_, "SYA", syE2b)
                for i in range(nDB):
                    if i >= 1:
                        W(t_, "VC", vcE2a + i)
                    t_.matmul(out=pd1[0][i % nDB][:, :], lhsT=onesF_1x128[:],
                              rhs=borow[:, i * NMM:(i + 1) * NMM], start=True, stop=True
                              ).then_inc(sem["TC"], 1)
            C.inc("TC", nDB)

            @block.vector
            def _(v):
                bar_waits(v, barE2)
                for i in range(nDB):
                    W(v, "TC", tcE2a + i + 1)
                    v.tensor_copy(out=boutB[:, i * NMM:(i + 1) * NMM], in_=pd1[0][i % nDB][:, :]
                                  ).then_inc(sem["VC"], 1)
            C.inc("VC", nDB)
            vcE2b = C.v["VC"]

            syE2c = [C.v["SY0"], C.v["SY1"]]
            tcE2c, vcE2c = C.v["TC"], C.v["VC"]
            scE2c = [C.v["SC0"], C.v["SC1"]]

            @block.sync
            def _(sync):
                for t in range(nT):
                    s = t % 2
                    if t >= 2:
                        W(sync, "TC", tcE2c + t - 1)
                    sync.dma_start(out=lblk1[s][:], in_=OTT[:, t].transpose([1, 0, 2])
                                   ).then_inc(sem["SY" + str(s)], 16)
            C.inc("SY0", 16 * ((nT + 1) // 2))
            C.inc("SY1", 16 * (nT // 2))

            @block.tensor
            def _(t_):
                W(t_, "GPA", gpE2)
                W(t_, "VC", vcE2b)
                for t in range(nT):
                    s = t % 2
                    W(t_, "SY" + str(s), syE2c[s] + 16 * (t // 2 + 1))
                    if t >= 2:
                        W(t_, "VC", vcE2b + t - 1)
                    mm = None
                    for ei in range(nD):
                        for b in range(nDB):
                            mm = t_.matmul(out=pd1[s][b][:, :],
                                           lhsT=lblk1[s][:, ei * 128:(ei + 1) * 128],
                                           rhs=wstr1[ei][:, b * NMM:(b + 1) * NMM],
                                           start=(ei == 0), stop=(ei == nD - 1))
                    mm.then_inc(sem["TC"], 1)
            C.inc("TC", nT)

            @block.vector
            def _(v):
                for t in range(nT):
                    s = t % 2
                    W(v, "TC", tcE2c + t + 1)
                    if t >= 2:
                        W(v, "SC" + str(s), scE2c[s] + 16 * ((t - 2) // 2 + 1))
                    for b in range(nDB):
                        v.tensor_tensor(out=s1b[s][:, b * NMM:(b + 1) * NMM],
                                        in0=pd1[s][b][:, :],
                                        in1=boutB[:, b * NMM:(b + 1) * NMM], op=ALU.add)
                    v.engine_nop().then_inc(sem["VC"], 1)
            C.inc("VC", nT)

            @block.scalar
            def _(s_):
                bar_waits(s_, barE2)
                for t in range(nT):
                    s = t % 2
                    W(s_, "VC", vcE2b + t + 1)
                    s_.dma_start(out=S1[t * 128:(t + 1) * 128, :], in_=s1b[s][:]
                                 ).then_inc(sem["SC" + str(s)], 16)
            C.inc("SC0", 16 * ((nT + 1) // 2))
            C.inc("SC1", 16 * (nT // 2))

        def stage_ternquant(tag, WSRC, QDST, ntile, width, ngrp):
            bar = dict(C.v)
            with ExitStack() as sq:
                qsb = lambda n, sh, dt=F32: sq.enter_context(nc.sbuf_tensor(n, sh, dt))
                wt = [qsb(f"twt{tag}{i}", [128, width], BF16) for i in range(2)]
                gs = [qsb(f"tgs{tag}{i}", [128, ngrp]) for i in range(2)]
                scb = [qsb(f"tsc{tag}{i}", [128, ngrp], BF16) for i in range(2)]
                thr = [qsb(f"tth{tag}{i}", [128, ngrp]) for i in range(2)]
                nthr = [qsb(f"tnt{tag}{i}", [128, ngrp]) for i in range(2)]
                gtb = [qsb(f"tgt{tag}{i}", [128, width], BF16) for i in range(2)]
                ltb = [qsb(f"tlt{tag}{i}", [128, width], BF16) for i in range(2)]
                wqb = [qsb(f"twq{tag}{i}", [128, width], BF16) for i in range(2)]
                vc0, gc0 = C.v["VC"], C.v["GC"]
                gp0 = [C.v["GP0"], C.v["GP1"]]
                sc0 = [C.v["SC0"], C.v["SC1"]]

                def gview(ap):
                    return ap.rearrange("p (g c) -> p g c", c=64)

                @block.gpsimd
                def _(g):
                    bar_waits(g, bar)
                    for k in range(ntile):
                        b = k % 2
                        if k >= 2:
                            W(g, "VC", vc0 + 2 * (k - 2) + 1)
                        g.dma_start(out=wt[b][:], in_=WSRC[k * 128:(k + 1) * 128, :]
                                    ).then_inc(sem["GP" + str(b)], 16)
                C.inc("GP0", 16 * ((ntile + 1) // 2))
                C.inc("GP1", 16 * (ntile // 2))

                @block.vector
                def _(v):
                    bar_waits(v, bar)
                    for k in range(ntile):
                        b = k % 2
                        W(v, "GP" + str(b), gp0[b] + 16 * (k // 2 + 1))
                        v.tensor_reduce(out=gs[b][:], in_=gview(wt[b][:]), axis=AX, op=ALU.add,
                                        apply_absolute_value=True).then_inc(sem["VSV"], 1)
                        v.wait_ge(sem["VSV"], C.inc("VSV"))
                        VN(v, v.tensor_scalar(out=scb[b][:], in0=gs[b][:], scalar1=1.0 / 64,
                                              scalar2=1e-8, op0=ALU.mult, op1=ALU.max))
                        v.tensor_scalar(out=thr[b][:], in0=scb[b][:], scalar1=THRC,
                                        scalar2=None, op0=ALU.mult)
                        VN(v, v.tensor_scalar(out=nthr[b][:], in0=scb[b][:], scalar1=-THRC,
                                              scalar2=None, op0=ALU.mult))
                        v.tensor_tensor(out=gview(gtb[b][:]), in0=gview(wt[b][:]),
                                        in1=thr[b][:].unsqueeze(2).broadcast_to([128, ngrp, 64]),
                                        op=ALU.is_gt)
                        v.tensor_tensor(out=gview(ltb[b][:]), in0=gview(wt[b][:]),
                                        in1=nthr[b][:].unsqueeze(2).broadcast_to([128, ngrp, 64]),
                                        op=ALU.is_lt).then_inc(sem["VC"], 1)
                        v.wait_ge(sem["VC"], vc0 + 2 * k + 1)
                        if k >= 2:
                            W(v, "SC" + str(b), sc0[b] + 16 * ((k - 2) // 2 + 1))
                        VN(v, v.tensor_tensor(out=gtb[b][:], in0=gtb[b][:], in1=ltb[b][:],
                                              op=ALU.subtract))
                        v.tensor_tensor(out=gview(wqb[b][:]), in0=gview(gtb[b][:]),
                                        in1=scb[b][:].unsqueeze(2).broadcast_to([128, ngrp, 64]),
                                        op=ALU.mult).then_inc(sem["VC"], 1)
                C.inc("VC", 2 * ntile)

                @block.scalar
                def _(s_):
                    bar_waits(s_, bar)
                    for k in range(ntile):
                        b = k % 2
                        W(s_, "VC", vc0 + 2 * k + 2)
                        s_.dma_start(out=QDST[k * 128:(k + 1) * 128, :], in_=wqb[b][:]
                                     ).then_inc(sem["SC" + str(b)], 16)
                C.inc("SC0", 16 * ((ntile + 1) // 2))
                C.inc("SC1", 16 * (ntile // 2))

        stage_ternquant("f0a", WFC, WQ1N, nH, D, nG1)
        stage_transpose("f0at", WQ1N, WQ1T, H, D)
        stage_ternquant("f0b", WPR, WQ2N, nD, H, nG2)
        stage_transpose("f0bt", WQ2N, WQ2T, D, H)

        barF1 = dict(C.v)
        with ExitStack() as sF1:
            fsb = lambda n, sh, dt=F32: sF1.enter_context(nc.sbuf_tensor(n, sh, dt))
            rstdB = fsb("rstdB", [128, TN], BF16)

            with ExitStack() as sF1i:
                fsbi = lambda n, sh, dt=F32: sF1i.enter_context(nc.sbuf_tensor(n, sh, dt))
                xn1Tb = [fsbi(f"xn1Tb{i}", [128, TN], BF16) for i in range(nD)]
                wbkF = [fsbi(f"wbkF{i}", [128, nD * 128], BF16) for i in range(2)]
                raF = [[fsbi(f"raF{s}{b}", [128, TOKB]) for b in range(nBQ)] for s in range(2)]
                rbF = [[fsbi(f"rbF{s}{b}", [128, TOKB], BF16) for b in range(nBQ)] for s in range(2)]
                r2F = [[fsbi(f"r2F{s}{b}", [128, TOKB], BF16) for b in range(nBQ)] for s in range(2)]
                msrow = fsbi("msrow", [1, TN])
                na0 = fsbi("na0F", [1, NMM])
                na1 = fsbi("na1F", [1, NMM])
                na2 = fsbi("na2F", [1, NMM])
                rsbrow = fsbi("rsbrow", [1, TN], BF16)
                ppF = [[sF1i.enter_context(nc.psum_tensor(f"ppF{s}{b}", [128, TOKB], F32))
                        for b in range(nBQ)] for s in range(2)]
                pmsF = [sF1i.enter_context(nc.psum_tensor(f"pmsF{b}", [1, TOKB], F32))
                        for b in range(nBQ)]
                pbcF = [sF1i.enter_context(nc.psum_tensor(f"pbcF{i}", [128, NMM], F32))
                        for i in range(2)]

                @block.sync
                def _(sync):
                    bar_waits(sync, barF1)
                    for di in range(nD):
                        sync.dma_start(out=xn1Tb[di][:], in_=XN1[0:TN, di * 128:(di + 1) * 128],
                                       transpose=True).then_inc(sem["SYA"], 16)
                C.inc("SYA", 16 * nD)
                syF1a = C.v["SYA"]

                tcF0, vcF0, acF0 = C.v["TC"], C.v["VC"], C.v["AC"]
                gpF1w = [C.v["GP0"], C.v["GP1"]]
                scF0 = [C.v["SC0"], C.v["SC1"]]
                nit = nQ4 * nH

                def vcount(i):
                    q4, h = divmod(i, nH)
                    return vcF0 + q4 * (nH + 1) + h

                def tc_mm(i):
                    return tcF0 + (1 if i == 0 else 2 * i)

                def tc_mm2(j):
                    return tcF0 + (2 * (j + 1) + 1 if j < nit - 1 else 2 * nit)

                def emit_mm2(t_, j):
                    q4j, hj = divmod(j, nH)
                    sj = j % 2
                    W(t_, "VC", vcount(j) + 1)
                    mm2 = None
                    for b in range(nBQ):
                        mm2 = t_.matmul(out=pmsF[b][:, :], lhsT=onesB_128x1[:],
                                        rhs=r2F[sj][b][:], start=(hj == 0),
                                        stop=(hj == nH - 1))
                    mm2.then_inc(sem["TC"], 1)


                @block.gpsimd
                def _(g):
                    bar_waits(g, barF1)
                    for q4 in range(nQ4):
                        for h in range(nH):
                            i = q4 * nH + h
                            if i >= 2:
                                W(g, "TC", tc_mm(i - 2))
                            g.dma_start(out=wbkF[i % 2][:],
                                        in_=WQ1T[:, h].transpose([1, 0, 2])
                                        ).then_inc(sem["GP" + str(i % 2)], 16)
                C.inc("GP0", 16 * ((nit + 1) // 2))
                C.inc("GP1", 16 * (nit // 2))

                @block.tensor
                def _(t_):
                    bar_waits(t_, barF1)
                    W(t_, "SYA", syF1a)
                    for q4 in range(nQ4):
                        for h in range(nH):
                            i = q4 * nH + h
                            s = i % 2
                            W(t_, "GP" + str(s), gpF1w[s] + 16 * (i // 2 + 1))
                            if i >= 2:
                                W(t_, "AC", acF0 + (i - 1))
                            mm = None
                            for b in range(nBQ):
                                off = q4 * Q4 + b * TOKB
                                for di in range(nD):
                                    mm = t_.matmul(out=ppF[s][b][:, :],
                                                   lhsT=wbkF[s][:, di * 128:(di + 1) * 128],
                                                   rhs=xn1Tb[di][:, off:off + TOKB],
                                                   start=(di == 0), stop=(di == nD - 1))
                            mm.then_inc(sem["TC"], 1)
                            if i >= 1:
                                emit_mm2(t_, i - 1)
                    emit_mm2(t_, nit - 1)
                C.inc("TC", 2 * nit)

                @block.scalar
                def _(s_):
                    bar_waits(s_, barF1)
                    for q4 in range(nQ4):
                        for h in range(nH):
                            i = q4 * nH + h
                            s = i % 2
                            W(s_, "TC", tc_mm(i))
                            if i >= 2:
                                W(s_, "VC", vcount(i - 2) + 1)
                                W(s_, "SC" + str(s), scF0[s] + 32 * ((i - 2) // 2 + 1))
                            li = None
                            for b in range(nBQ):
                                li = s_.activation(out=raF[s][b][:], in_=ppF[s][b][:, :], func=AF.Relu)
                            VN(s_, li)
                            for b in range(nBQ):
                                li = s_.activation(out=rbF[s][b][:], in_=raF[s][b][:], func=AF.Square)
                            li.then_inc(sem["AC"], 1)
                            W(s_, "AC", acF0 + i + 1)
                            for b in range(nBQ):
                                off = q4 * Q4 + b * TOKB
                                s_.dma_start(out=RT[h * 128:(h + 1) * 128, off:off + TOKB],
                                             in_=rbF[s][b][:]).then_inc(sem["SC" + str(s)], 16)
                C.inc("AC", nit)
                C.inc("SC0", 32 * ((nit + 1) // 2))
                C.inc("SC1", 32 * (nit // 2))

                @block.vector
                def _(v):
                    bar_waits(v, barF1)
                    for q4 in range(nQ4):
                        for h in range(nH):
                            i = q4 * nH + h
                            s = i % 2
                            W(v, "AC", acF0 + i + 1)
                            if i >= 2:
                                W(v, "TC", tc_mm2(i - 2))
                            for b in range(nBQ):
                                v.tensor_tensor(out=r2F[s][b][:], in0=rbF[s][b][:],
                                                in1=rbF[s][b][:], op=ALU.mult)
                            v.engine_nop().then_inc(sem["VC"], 1)
                        W(v, "TC", tc_mm2((q4 + 1) * nH - 1))
                        for b in range(nBQ):
                            off = q4 * Q4 + b * TOKB
                            v.tensor_copy(out=msrow[:, off:off + TOKB], in_=pmsF[b][:, :])
                        v.engine_nop().then_inc(sem["VC"], 1)
                C.inc("VC", nQ4 * (nH + 1))
                vcF_ms = C.v["VC"]

                acF_rs0 = C.v["AC"]
                vcF_rs0 = C.v["VC"]
                tcF_b0 = C.v["TC"]
                nTB = TN // NMM

                @block.scalar
                def _(s_):
                    for tb in range(nTB):
                        W(s_, "VC", vcF_ms if tb == 0 else vcF_rs0 + 2 * tb)
                        sl = slice(tb * NMM, (tb + 1) * NMM)
                        VN(s_, s_.activation(out=na0[:], in_=msrow[:, sl], func=AF.Ln,
                                              scale=1.0 / H, bias=epsA1[0:1, 0:1]))
                        s_.activation(out=na1[:], in_=na0[:], func=AF.Exp, scale=-0.5
                                      ).then_inc(sem["AC"], 1)
                C.inc("AC", nTB)

                @block.vector
                def _(v):
                    for tb in range(nTB):
                        sl = slice(tb * NMM, (tb + 1) * NMM)
                        W(v, "AC", acF_rs0 + tb + 1)
                        VN(v, v.tensor_scalar(out=na0[:], in0=msrow[:, sl], scalar1=1.0 / H,
                                              scalar2=FEPS, op0=ALU.mult, op1=ALU.add))
                        VN(v, v.tensor_tensor(out=na2[:], in0=na1[:], in1=na1[:], op=ALU.mult))
                        VN(v, v.tensor_tensor(out=na2[:], in0=na2[:], in1=na0[:], op=ALU.mult))
                        VN(v, v.tensor_scalar(out=na2[:], in0=na2[:], scalar1=-0.5, scalar2=1.5,
                                              op0=ALU.mult, op1=ALU.add))
                        VN(v, v.tensor_tensor(out=na0[:], in0=na1[:], in1=na2[:], op=ALU.mult))
                        v.tensor_copy(out=rsbrow[:, sl], in_=na0[:]).then_inc(sem["VC"], 1)
                        W(v, "TC", tcF_b0 + tb + 1)
                        v.tensor_copy(out=rstdB[:, sl], in_=pbcF[tb % 2][:, :]
                                      ).then_inc(sem["VC"], 1)
                C.inc("VC", 2 * nTB)

                @block.tensor
                def _(t_):
                    for tb in range(nTB):
                        W(t_, "VC", vcF_rs0 + 2 * tb + 1)
                        t_.matmul(out=pbcF[tb % 2][:, :], lhsT=onesB_1x128[:],
                                  rhs=rsbrow[:, tb * NMM:(tb + 1) * NMM], start=True, stop=True
                                  ).then_inc(sem["TC"], 1)
                C.inc("TC", nTB)

            barF2 = dict(C.v)
            with ExitStack() as sF2:
                f2sb = lambda n, sh, dt=F32: sF2.enter_context(nc.sbuf_tensor(n, sh, dt))
                rtiS = [f2sb(f"rtiS{j}", [128, Q4], BF16) for j in range(nH)]
                wbk2 = [f2sb(f"wbkF2{i}", [128, nH * 128], BF16) for i in range(2)]
                ob2 = [f2sb(f"obF2{i}", [128, Q4], BF16) for i in range(2)]
                pp2 = [[sF2.enter_context(nc.psum_tensor(f"ppF2{s}{b}", [128, TOKB], F32))
                        for b in range(nBQ)] for s in range(2)]
                tc0, vc0 = C.v["TC"], C.v["VC"]
                sc0 = [C.v["SC0"], C.v["SC1"]]
                sy20 = C.v["SY2"]
                sy30 = C.v["SY3"]
                syW0 = C.v["SY0"]
                gpW0 = C.v["GP0"]
                nEvenH = (nH + 1) // 2
                nOddH = nH // 2

                @block.sync
                def _(sync):
                    bar_waits(sync, barF2)
                    for q4 in range(nQ4):
                        if q4 >= 1:
                            W(sync, "TC", tc0 + q4 * nD)
                        for hj in range(0, nH, 2):
                            sync.dma_start(out=rtiS[hj][:],
                                           in_=RT[hj * 128:(hj + 1) * 128, q4 * Q4:(q4 + 1) * Q4]
                                           ).then_inc(sem["SY2"], 16)
                        for dj in range(0, nD, 2):
                            i = q4 * nD + dj
                            if i >= 2:
                                W(sync, "TC", tc0 + i - 1)
                            sync.dma_start(out=wbk2[0][:],
                                           in_=WQ2T[:, dj].transpose([1, 0, 2])
                                           ).then_inc(sem["SY0"], 16)
                C.inc("SY2", 16 * nEvenH * nQ4)
                C.inc("SY0", 16 * ((nD + 1) // 2) * nQ4)

                @block.scalar
                def _(s_):
                    bar_waits(s_, barF2)
                    for q4 in range(nQ4):
                        if q4 >= 1:
                            W(s_, "TC", tc0 + q4 * nD)
                        for hj in range(1, nH, 2):
                            s_.dma_start(out=rtiS[hj][:],
                                         in_=RT[hj * 128:(hj + 1) * 128, q4 * Q4:(q4 + 1) * Q4]
                                         ).then_inc(sem["SY3"], 16)
                        for dj in range(nD):
                            W(s_, "VC", vc0 + q4 * nD + dj + 1)
                            s = (q4 * nD + dj) % 2
                            s_.dma_start(out=OUT1T[dj * 128:(dj + 1) * 128, q4 * Q4:(q4 + 1) * Q4],
                                         in_=ob2[s][:]).then_inc(sem["SC" + str(s)], 16)
                C.inc("SY3", 16 * nOddH * nQ4)
                C.inc("SC0", 16 * ((nQ4 * nD + 1) // 2))
                C.inc("SC1", 16 * ((nQ4 * nD) // 2))

                @block.gpsimd
                def _(g):
                    bar_waits(g, barF2)
                    for q4 in range(nQ4):
                        for dj in range(1, nD, 2):
                            i = q4 * nD + dj
                            if i >= 2:
                                W(g, "TC", tc0 + i - 1)
                            g.dma_start(out=wbk2[1][:],
                                        in_=WQ2T[:, dj].transpose([1, 0, 2])
                                        ).then_inc(sem["GP0"], 16)
                C.inc("GP0", 16 * (nD // 2) * nQ4)

                @block.tensor
                def _(t_):
                    bar_waits(t_, barF2)
                    for q4 in range(nQ4):
                        for dj in range(nD):
                            i = q4 * nD + dj
                            s = i % 2
                            if dj == 0:
                                W(t_, "SY2", sy20 + 16 * nEvenH * (q4 + 1))
                                W(t_, "SY3", sy30 + 16 * nOddH * (q4 + 1))
                            if dj % 2 == 0:
                                W(t_, "SY0", syW0 + 16 * (q4 * ((nD + 1) // 2) + dj // 2 + 1))
                            else:
                                W(t_, "GP0", gpW0 + 16 * (q4 * (nD // 2) + dj // 2 + 1))
                            if i >= 2:
                                W(t_, "VC", vc0 + i - 1)
                            mm = None
                            for b in range(nBQ):
                                for hj in range(nH):
                                    mm = t_.matmul(out=pp2[s][b][:, :],
                                                   lhsT=wbk2[dj % 2][:, hj * 128:(hj + 1) * 128],
                                                   rhs=rtiS[hj][:, b * TOKB:(b + 1) * TOKB],
                                                   start=(hj == 0), stop=(hj == nH - 1))
                            mm.then_inc(sem["TC"], 1)
                C.inc("TC", nQ4 * nD)

                @block.vector
                def _(v):
                    bar_waits(v, barF2)
                    for q4 in range(nQ4):
                        for dj in range(nD):
                            i = q4 * nD + dj
                            s = i % 2
                            W(v, "TC", tc0 + i + 1)
                            if i >= 2:
                                W(v, "SC" + str(s), sc0[s] + 16 * ((i - 2) // 2 + 1))
                            for b in range(nBQ):
                                v.tensor_tensor(out=ob2[s][:, b * TOKB:(b + 1) * TOKB],
                                                in0=pp2[s][b][:, :],
                                                in1=rstdB[:, q4 * Q4 + b * TOKB:
                                                          q4 * Q4 + (b + 1) * TOKB],
                                                op=ALU.mult)
                            v.engine_nop().then_inc(sem["VC"], 1)
                C.inc("VC", nQ4 * nD)

        barG = dict(C.v)
        with ExitStack() as sG:
            gsb = lambda n, sh, dt=F32: sG.enter_context(nc.sbuf_tensor(n, sh, dt))
            x1t = [gsb(f"x1tG{i}", [128, D]) for i in range(2)]
            s1t = [gsb(f"s1tG{i}", [128, D], BF16) for i in range(2)]
            o1T = [gsb(f"o1TG{i}", [128, D], BF16) for i in range(2)]
            g1G = [gsb(f"g1G{i}", [128, D], BF16) for i in range(2)]
            sgG = gsb("sgG", [128, D])
            yb = [gsb(f"ybG{i}", [128, D]) for i in range(2)]
            ac0, vc0 = C.v["AC"], C.v["VC"]
            sy0 = [C.v["SY0"], C.v["SY1"]]
            gpG0 = [C.v["GP0"], C.v["GP1"]]
            sc0 = [C.v["SC0"], C.v["SC1"]]

            @block.sync
            def _(sync):
                bar_waits(sync, barG)
                for t in range(nT):
                    b = t % 2
                    if t >= 2:
                        W(sync, "VC", vc0 + (t - 1))
                        W(sync, "AC", ac0 + (t - 1))
                    sync.dma_start(out=x1t[b][:], in_=X1[t * 128:(t + 1) * 128, :]).then_inc(sem["SY" + str(b)], 16)
                    sync.dma_start(out=o1T[b][:], in_=OUT1T[0:D, t * 128:(t + 1) * 128],
                                   transpose=True).then_inc(sem["SY" + str(b)], 16)
            C.inc("SY0", 32 * ((nT + 1) // 2))
            C.inc("SY1", 32 * (nT // 2))

            @block.gpsimd
            def _(g):
                bar_waits(g, barG)
                for t in range(nT):
                    b = t % 2
                    if t >= 2:
                        W(g, "VC", vc0 + (t - 1))
                    g.dma_start(out=s1t[b][:], in_=S1[t * 128:(t + 1) * 128, :]).then_inc(sem["GP" + str(b)], 16)
            C.inc("GP0", 16 * ((nT + 1) // 2))
            C.inc("GP1", 16 * (nT // 2))

            @block.scalar
            def _(s_):
                bar_waits(s_, barG)
                for t in range(nT):
                    b = t % 2
                    W(s_, "SY" + str(b), sy0[b] + 32 * (t // 2 + 1))
                    s_.activation(out=g1G[b][:], in_=o1T[b][:], func=AF.Silu).then_inc(sem["AC"], 1)
                    W(s_, "VC", vc0 + t + 1)
                    s_.dma_start(out=Y[t * 128:(t + 1) * 128, :], in_=yb[b][:]).then_inc(sem["SC" + str(b)], 16)
            C.inc("AC", nT)
            C.inc("SC0", 16 * ((nT + 1) // 2))
            C.inc("SC1", 16 * (nT // 2))

            @block.vector
            def _(v):
                bar_waits(v, barG)
                for t in range(nT):
                    b = t % 2
                    W(v, "AC", ac0 + t + 1)
                    W(v, "GP" + str(b), gpG0[b] + 16 * (t // 2 + 1))
                    if t >= 2:
                        W(v, "SC" + str(b), sc0[b] + 16 * ((t - 2) // 2 + 1))
                    VN(v, v.tensor_tensor(out=sgG[:], in0=g1G[b][:], in1=s1t[b][:], op=ALU.mult))
                    v.tensor_tensor(out=yb[b][:], in0=sgG[:], in1=x1t[b][:], op=ALU.add
                                    ).then_inc(sem["VC"], 1)
            C.inc("VC", nT)

        barEnd = dict(C.v)

        tap_map = dict(XN=XN, XN2=XN2, XQ0=XQ0, OUT0T=OUT0T, S0=S0, H0T=H0T,
                       XQ2=XQ2, OF0T=OF0T, X1=X1, XN1=XN1, QT=QT, KT=KT,
                       OTT=OTT, S1=S1, RT=RT, OUT1T=OUT1T, T1N=T1N, T2N=T2N,
                       WQ1N=WQ1N, WQ2N=WQ2N, DTWD=DTWD)
        ntaps = 0

        @block.sync
        def _(sync):
            bar_waits(sync, barEnd)
            for tp in taps:
                ap = tap_map[tp]
                dt_ = ap.dtype
                sh = list(ap.shape)
                tout = nc.dram_tensor("tap_" + tp, sh, dt_, kind="ExternalOutput").ap()
                sync.dma_start(out=tout, in_=ap).then_inc(sem["SYA"], 16)
            if taps:
                sync.wait_ge(sem["SYA"], barEnd["SYA"] + 16 * len(taps))

        @block.vector
        def _(v):
            bar_waits(v, barEnd)

        @block.scalar
        def _(s_):
            bar_waits(s_, barEnd)

        @block.tensor
        def _(t_):
            bar_waits(t_, barEnd)

        @block.gpsimd
        def _(g):
            bar_waits(g, barEnd)

    return nc


def _prep_inputs(cfg, x_slice, gamma, beta, W_in, W_out, dt, w1, w2,
                 w_qkv, b_qkv, w_out1, b_out1, W_fc, W_proj):
    c = _derive(cfg)
    nD, nE0, nQKV, nDI = c["nD"], c["nE0"], c["nQKV"], c["nDI"]
    f = lambda a: np.ascontiguousarray(np.asarray(a, dtype=np.float32))
    return {
        "x": f(x_slice),
        "w_inT": f(np.asarray(W_in).T.reshape(nD, 128, nE0, 128).transpose(2, 0, 1, 3)),
        "w_outT0": f(np.asarray(W_out).T.reshape(nDI, 128, c["D"])),
        "w1": f(w1), "w2": f(w2), "dt": f(dt), "gamma": f(gamma), "beta": f(beta),
        "wqkvT": f(np.asarray(w_qkv).T.reshape(nD, 128, nQKV, 128).transpose(2, 0, 1, 3)),
        "bqkv": f(b_qkv),
        "w_outT1": f(np.asarray(w_out1).T.reshape(nD, 128, c["D"])),
        "bout1": f(b_out1),
        "wfc": f(W_fc), "wproj": f(W_proj),
    }


def kernel(L0_attn_x, L0_attn_gamma, L0_attn_beta, L0_attn_W_in, L0_attn_W_out,
           L0_attn_dt, L0_ffn_w1, L0_ffn_w2, L1_attn_w_qkv, L1_attn_b_qkv,
           L1_attn_w_out, L1_attn_b_out, L1_ffn_W_fc, L1_ffn_W_proj):
    cfg = CFG_FULL
    c = _derive(cfg)
    x = np.asarray(L0_attn_x, dtype=np.float32)
    B = x.shape[0]
    per = B // N_CORES
    nc = build(cfg)
    in_maps = []
    for core in range(N_CORES):
        xs = x[core * per:(core + 1) * per].reshape(c["TN"], c["D"])
        in_maps.append(_prep_inputs(
            cfg, xs, L0_attn_gamma, L0_attn_beta, L0_attn_W_in, L0_attn_W_out,
            L0_attn_dt, L0_ffn_w1, L0_ffn_w2, L1_attn_w_qkv, L1_attn_b_qkv,
            L1_attn_w_out, L1_attn_b_out, L1_ffn_W_fc, L1_ffn_W_proj))
    res = run_bass_kernel_spmd(nc, in_maps, core_ids=list(range(N_CORES)))
    outs = [np.asarray(r["y"]).reshape(per, cfg["S"], cfg["D"]) for r in res.results]
    return np.concatenate(outs, axis=0)
